# version 12
# baseline (speedup 1.0000x reference)
import math
import contextlib
import numpy as np
import ml_dtypes
import concourse.bass as bass
import concourse.mybir as mybir
from concourse.bass_utils import run_bass_kernel_spmd

F32 = mybir.dt.float32
BF16 = mybir.dt.bfloat16
I32 = mybir.dt.int32
AF = mybir.ActivationFunctionType
ALU = mybir.AluOpType
AXX = mybir.AxisListType.X
IOA = bass.IndirectOffsetOnAxis

BFNP = ml_dtypes.bfloat16
SUMSCALE = float(2 ** 26)
DEBUG = [False]


class Cfg:
    def __init__(self, TOK, D, NH):
        self.NC = 8
        self.TOK = TOK
        self.T = 8 * TOK
        self.D = D
        self.NH = NH
        self.NKV = 8
        self.HD = 128
        self.G = NH // 8
        self.KCH = D // 128
        self.TCH = TOK // 128
        self.QB = 512
        self.NB = self.T // self.QB
        self.DPB = self.QB // 128
        self.PTB = min(512, TOK)
        self.NTB = max(1, TOK // 512)
        self.RANKS = (8, 16, 32, 64)
        self.EPS = 1e-5
        self.SCALE = 1.0 / math.sqrt(self.HD)
        assert D == NH * 128
        assert TOK % 128 == 0 and self.T % self.QB == 0


REAL = Cfg(TOK=1024, D=4096, NH=32)


def rope_tables(T):
    HD = 128
    theta, factor, high_ff, low_ff, orig_max = 500000.0, 8.0, 4.0, 1.0, 8192
    inv = 1.0 / theta ** (np.arange(0, HD, 2, dtype=np.float32) / HD)
    wavelen = 2.0 * np.pi / inv
    low_wl = orig_max / low_ff
    high_wl = orig_max / high_ff
    inv_scaled = np.where(wavelen > low_wl, inv / factor, inv)
    smooth = (orig_max / wavelen - low_ff) / (high_ff - low_ff)
    smoothed = (1.0 - smooth) / factor * inv + smooth * inv
    is_medium = (wavelen <= low_wl) & (wavelen >= high_wl)
    invf = np.where(is_medium, smoothed, inv_scaled).astype(np.float32)
    pos = np.arange(T, dtype=np.float32)
    freqs = pos[:, None] * invf[None, :]
    emb = np.concatenate([freqs, freqs], axis=-1)
    return np.cos(emb).astype(np.float32), np.sin(emb).astype(np.float32)


class Sem:
    __slots__ = ("h", "n")

    def __init__(self, h):
        self.h = h
        self.n = 0


class Rec:
    ENGS = ("sync", "scalar", "vector", "tensor", "gpsimd")

    def __init__(self, nc):
        self.nc = nc
        self.q = {e: [] for e in self.ENGS}
        self._stack = contextlib.ExitStack()
        self.sems = {}

    def sem(self, name):
        if name not in self.sems:
            self.sems[name] = Sem(self._stack.enter_context(self.nc.semaphore(name)))
        return self.sems[name]

    def op(self, eng, fn, wait=(), inc=None, inc_amt=1):
        wait = [(s, v) for (s, v) in wait if v > 0]
        for s, v in wait:
            assert v <= s.n, f"wait on unrecorded value {v} > {s.n}"
        if inc is not None:
            inc.n += inc_amt

        def run(e, wait=wait, inc=inc, inc_amt=inc_amt, fn=fn):
            for s, v in wait:
                e.wait_ge(s.h, v)
            inst = fn(e)
            if inc is not None:
                inst.then_inc(inc.h, inc_amt)
        self.q[eng].append(run)

    def waitonly(self, eng, wait):
        wait = [(s, v) for (s, v) in wait if v > 0]
        for s, v in wait:
            assert v <= s.n

        def run(e, wait=wait):
            for s, v in wait:
                e.wait_ge(s.h, v)
        self.q[eng].append(run)

    def barrier(self):
        s = self.sem("sPH")
        base = s.n
        for eng in self.ENGS:
            self.op(eng, lambda e, s=s: e.sem_inc(s.h, 1))
            s.n += 1
        for eng in self.ENGS:
            self.waitonly(eng, [(s, base + 5)])


def build_program(cfg):
    nc = bass.Bass("TRN2", target_bir_lowering=False)
    c = cfg
    TOK, D, KCH, TCH, T = c.TOK, c.D, c.KCH, c.TCH, c.T
    NB, QB, DPB, G = c.NB, c.QB, c.DPB, c.G
    PTB, NTB = c.PTB, c.NTB
    KVSH = 128 * TOK

    dt = nc.dram_tensor
    x_in = dt("x_in", [TOK, D], F32, kind="ExternalInput").ap()
    Wq = dt("Wq", [D, D], F32, kind="ExternalInput").ap()
    Wk = dt("Wk", [1024, D], F32, kind="ExternalInput").ap()
    Wv = dt("Wv", [1024, D], F32, kind="ExternalInput").ap()
    Wo = dt("Wo", [D, D], F32, kind="ExternalInput").ap()
    Wf = dt("Wf", [D, D], F32, kind="ExternalInput").ap()
    bf_in = dt("bf_in", [D], F32, kind="ExternalInput").ap()
    Af = dt("Af", [64, D], F32, kind="ExternalInput").ap()
    Bf = dt("Bf", [D, 64], F32, kind="ExternalInput").ap()
    gw = dt("gw", [4, D], F32, kind="ExternalInput").ap()
    gb = dt("gb", [4], F32, kind="ExternalInput").ap()
    cosd = dt("cosd", [128, TOK], F32, kind="ExternalInput").ap()
    sind = dt("sind", [128, TOK], F32, kind="ExternalInput").ap()
    id32d = dt("id32d", [128, 128], F32, kind="ExternalInput").ap()
    idbd = dt("idbd", [128, 128], BF16, kind="ExternalInput").ap()
    trid = dt("trid", [128, 128], BF16, kind="ExternalInput").ap()
    mkd = dt("mkd", [4, 64], BF16, kind="ExternalInput").ap()
    idxKd = dt("idxKd", [128, 1], I32, kind="ExternalInput").ap()
    idxOd = dt("idxOd", [128, 1], I32, kind="ExternalInput").ap()
    idxVd = dt("idxVd", [128, TCH], I32, kind="ExternalInput").ap()
    out_d = dt("out_d", [TOK, D], F32, kind="ExternalOutput").ap()
    if DEBUG[0] & 1:
        dbg_z1 = dt("dbg_z1", [TOK, D], BF16, kind="ExternalOutput").ap()
    if DEBUG[0] & 2:
        dbg_znt = dt("dbg_znt", [128, KCH * TOK], BF16, kind="ExternalOutput").ap()
    if DEBUG[0] & 4:
        dbg_att = dt("dbg_att", [128, KCH * TOK], BF16, kind="ExternalOutput").ap()
    if DEBUG[0] & 8:
        dbg_htm = dt("dbg_htm", [128, TCH * D], BF16, kind="ExternalOutput").ap()
    if DEBUG[0] & 16:
        dbg_um = dt("dbg_um", [64, TOK], BF16, kind="ExternalOutput").ap()
    if DEBUG[0] & 32:
        dbg_gt = dt("dbg_gt", [128, 4 * TCH], F32, kind="ExternalOutput").ap()
    if DEBUG[0] & 64:
        dbg_if = dt("dbg_if", [4, TOK], BF16, kind="ExternalOutput").ap()
    if DEBUG[0] & 128:
        dbg_ug = dt("dbg_ug", [68, TOK], F32, kind="ExternalOutput").ap()

    pay_kv = dt("pay_kv", [1, 16 * KVSH], BF16, kind="Internal").ap()
    gout_kv = dt("gout_kv", [8, 16 * KVSH], BF16, kind="Internal").ap()
    pay_q = [dt(f"pay_q{u}", [1, 8 * KVSH], BF16, kind="Internal").ap() for u in range(G)]
    gout_q = [dt(f"gout_q{u}", [8, 8 * KVSH], BF16, kind="Internal").ap() for u in range(G)]
    NPd = max(1, (8 * TOK) // c.QB // 4)
    TPPd = (8 * TOK) // NPd
    pay_o = [dt(f"pay_o{u}", [1, 128 * (8 * TOK)], BF16, kind="Internal").ap()
             for u in range(max(1, G - 1))]
    gout_o = [dt(f"gout_o{u}", [8, 128 * (8 * TOK)], BF16, kind="Internal").ap()
              for u in range(max(1, G - 1))]
    pay_ol = [dt(f"pay_ol{p}", [1, 128 * TPPd], BF16, kind="Internal").ap()
              for p in range(NPd)]
    gout_ol = [dt(f"gout_ol{p}", [8, 128 * TPPd], BF16, kind="Internal").ap()
               for p in range(NPd)]
    gassm = dt("gassm", [8, 128 * (8 * TOK)], BF16, kind="Internal").ap()
    z1d = dt("z1d", [TOK, D], BF16, kind="Internal").ap()

    r = Rec(nc)
    sLDX = r.sem("sLDX")
    sLX = [r.sem("sLX0"), r.sem("sLX1")]
    sLDW = r.sem("sLDW")
    sCON = r.sem("sCON")
    sGPD = r.sem("sGPD")
    sKVX = r.sem("sKVX")
    sVX = r.sem("sVX")
    sQX = [r.sem("sQX0"), r.sem("sQX1")]
    sOX = r.sem("sOX")
    sZ1 = r.sem("sZ1")
    sASM = r.sem("sASM")
    sCC = r.sem("sCC")
    sTE = r.sem("sTE")
    sACT = r.sem("sACT")
    sDVE = r.sem("sDVE")
    sPOOL = r.sem("sPOOL")
    sP = [[r.sem(f"sP{gg}_{pp}") for pp in range(2)] for gg in range(6)]
    sST = r.sem("sST")

    stack = r._stack
    alloc = lambda name, shape, dtp: stack.enter_context(nc.sbuf_tensor(name, shape, dtp))

    BIG8 = alloc("BIG8", [128, KCH * TOK], BF16)
    id32 = alloc("id32", [128, 128], F32)
    idb = alloc("idb", [128, 128], BF16)
    tri = alloc("tri", [128, 128], BF16)
    mk = alloc("mk", [4, 64], BF16)
    onesb = alloc("onesb", [1, 128], BF16)
    onescol = alloc("onescol", [128, 1], BF16)
    onesS = alloc("onesS", [128, 128], BF16)
    bcolv = alloc("bcolv", [128, KCH], F32)
    gbias = alloc("gbias", [68, 1], F32)
    NP = max(1, NB // 4)
    BPP = NB // NP
    TPP = T // NP
    idxK = alloc("idxK", [128, 1], I32)
    idxO = alloc("idxO", [128, 1], I32)
    idxV = alloc("idxV", [128, TCH], I32)
    SQseg = [alloc(f"SQseg{i}", [128, 512], BF16) for i in range(2)]
    SQd = alloc("SQd", [128, D // 2], BF16)
    sumsqb = [alloc(f"sumsqb{i}", [128, 1], F32) for i in range(2)]
    ssum = [alloc(f"ssum{i}", [128, 1], F32) for i in range(2)]
    sumsq = [alloc(f"sumsq{i}", [128, 1], F32) for i in range(2)]
    meanc = [alloc(f"meanc{i}", [128, 1], F32) for i in range(2)]
    m2c = [alloc(f"m2c{i}", [128, 1], F32) for i in range(2)]
    varc = [alloc(f"varc{i}", [128, 1], F32) for i in range(2)]
    lnvc = [alloc(f"lnvc{i}", [128, 1], F32) for i in range(2)]
    rstdc = [alloc(f"rstdc{i}", [128, 1], F32) for i in range(2)]
    negmrc = [alloc(f"negmrc{i}", [128, 1], F32) for i in range(2)]
    segrow = alloc("segrow", [1, 512], BF16)
    segmax = alloc("segmax", [1, 1], F32)
    kn2m = alloc("kn2m", [1, 1], F32)
    qn2m = alloc("qn2m", [1, 1], F32)
    mtmp = alloc("mtmp", [1, 1], F32)
    mtmp2 = alloc("mtmp2", [1, 1], F32)
    negmb = alloc("negmb", [1, 1], BF16)
    negmcol = alloc("negmcol", [128, G], F32)

    PSA = stack.enter_context(nc.psum_tensor("PSA", [128, 1024], F32))
    PSB = stack.enter_context(nc.psum_tensor("PSB", [128, 1024], F32))
    PSC = stack.enter_context(nc.psum_tensor("PSC", [128, 1024], F32))
    PSE = stack.enter_context(nc.psum_tensor("PSE", [128, 512], F32))
    PSDh = [None]

    p12 = contextlib.ExitStack()
    a12 = lambda name, shape, dtp: p12.enter_context(nc.sbuf_tensor(name, shape, dtp))
    KTc = a12("KTc", [128, T], BF16)
    Vc = a12("Vc", [128, T], BF16)
    Qb0 = a12("Qb0", [128, T], BF16)
    p1 = contextlib.ExitStack()
    a1 = lambda name, shape, dtp: p1.enter_context(nc.sbuf_tensor(name, shape, dtp))
    PSD1 = p1.enter_context(nc.psum_tensor("PSD1", [128, 1024], BF16))
    PSDh[0] = PSD1
    cosT = a1("cosT", [128, TOK], F32)
    sinT = a1("sinT", [128, TOK], F32)
    xt = [a1(f"xt{i}", [128, D], F32) for i in range(2)]
    xnb0 = a1("xnb0", [128, D], BF16)
    xnb = [xnb0, xnb0]
    wnat = a1("wnat", [128, D], F32)
    lhsT = [a1(f"lhsT{i}", [128, KCH * 128], BF16) for i in range(2)]
    STG = [a1(f"STG{i}", [128, TOK], BF16) for i in range(2)]
    RT1 = a1("RT1", [128, TOK], BF16)
    VTM = RT1

    for dst, src in [(id32, id32d), (idb, idbd), (tri, trid), (mk, mkd),
                     (cosT, cosd), (sinT, sind), (idxK, idxKd), (idxO, idxOd),
                     (idxV, idxVd)]:
        r.op("sync", lambda e, dst=dst, src=src: e.dma_start(out=dst[:, :], in_=src[:, :]),
             inc=sCON, inc_amt=16)
    r.op("sync", lambda e: e.dma_start(out=bcolv[:, :],
                                       in_=bf_in.rearrange("(m p) -> p m", p=128),
                                       allow_slow_non_contiguous=True),
         inc=sCON, inc_amt=16)
    r.op("vector", lambda e: e.memset(gbias[:, :], 0.0), inc=sDVE)
    r.op("sync", lambda e: e.dma_start(out=gbias[64:68, 0:1],
                                       in_=gb.rearrange("(g o) -> g o", o=1)),
         wait=[(sDVE, sDVE.n)], inc=sCON, inc_amt=16)
    r.op("vector", lambda e: e.memset(onesb[:, :], 1.0), inc=sDVE)
    r.op("vector", lambda e: e.memset(onescol[:, :], 1.0), inc=sDVE)
    r.op("vector", lambda e: e.memset(onesS[:, :], SUMSCALE), inc=sDVE)
    CONSTS_LDW = sCON.n
    CONSTS_DVE = sDVE.n

    ln_act_tok = {0: 0, 1: 0}
    ln_dve_tok = {0: 0, 1: 0}
    ln_pool_tok = {0: 0, 1: 0}

    def ln_chain(src_big, apply_out, par=0, extra_act=(), extra_dve=(), extra_apply=()):
        p = par % 2
        su, sq, mn, m2, va, lv, rs, ng = (ssum[p], sumsq[p], meanc[p], m2c[p],
                                          varc[p], lnvc[p], rstdc[p], negmrc[p])
        sqb = sumsqb[p]
        r.op("vector", lambda e: e.reduce_sum(su[:, :], src_big, AXX),
             wait=[(sDVE, ln_dve_tok[p])] + list(extra_dve), inc=sDVE)
        r.op("scalar", lambda e: e.activation(SQd[:, :], src_big[:, 0:D // 2], AF.Square,
                                              bias=0.0, scale=1.0, accum_out=sq[:, 0:1]),
             wait=[(sACT, ln_act_tok[p]), (sDVE, ln_dve_tok[p])] + list(extra_act), inc=sACT)
        r.op("scalar", lambda e: e.activation(SQd[:, :], src_big[:, D // 2:D], AF.Square,
                                              bias=0.0, scale=1.0, accum_out=sqb[:, 0:1]),
             wait=[(sACT, sACT.n)], inc=sACT)
        SQTOK = sACT.n
        r.op("vector", lambda e: e.tensor_add(sq[:, :], sq[:, :], sqb[:, :]),
             wait=[(sACT, SQTOK), (sDVE, sDVE.n)], inc=sDVE)
        SQDVE = sDVE.n
        r.op("vector", lambda e: e.tensor_scalar_mul(mn[:, :], su[:, :], 1.0 / D),
             wait=[(sDVE, sDVE.n)], inc=sDVE)
        r.op("vector", lambda e: e.tensor_mul(m2[:, :], mn[:, :], mn[:, :]),
             wait=[(sDVE, sDVE.n)], inc=sDVE)
        r.op("vector", lambda e: e.tensor_scalar_add(m2[:, :], m2[:, :], -c.EPS),
             wait=[(sDVE, sDVE.n)], inc=sDVE)
        r.op("vector", lambda e: e.scalar_tensor_tensor(va[:, :], sq[:, :], 1.0 / D,
                                                        m2[:, :], ALU.mult, ALU.subtract),
             wait=[(sDVE, sDVE.n)], inc=sDVE)
        r.op("scalar", lambda e: e.activation(lv[:, :], va[:, :], AF.Ln, bias=0.0,
                                              scale=1.0),
             wait=[(sDVE, sDVE.n)], inc=sACT)
        r.op("scalar", lambda e: e.activation(rs[:, :], lv[:, :], AF.Exp, bias=0.0,
                                              scale=-0.5),
             wait=[(sACT, sACT.n), (sPOOL, ln_pool_tok[p])], inc=sACT)
        r.op("vector", lambda e: e.scalar_tensor_tensor(ng[:, :], mn[:, :], -1.0,
                                                        rs[:, :], ALU.mult, ALU.mult),
             wait=[(sACT, sACT.n), (sPOOL, ln_pool_tok[p])], inc=sDVE)
        r.op("gpsimd", lambda e: e.tensor_scalar(apply_out, src_big, rs[:, 0:1],
                                                 ng[:, 0:1], ALU.mult, ALU.add),
             wait=[(sDVE, sDVE.n), (sPOOL, ln_pool_tok[p])] + list(extra_apply), inc=sPOOL)
        ln_act_tok[p] = sACT.n
        ln_dve_tok[p] = sDVE.n
        ln_pool_tok[p] = sPOOL.n

    first_te = [True]

    def tm_to_fm(src_tile, dest_view_fn, i, wait_src):
        PSD = PSDh[0]
        for jg in range(KCH // 8):
            for jj in range(8):
                j = jg * 8 + jj
                w = []
                if jj == 0:
                    w = list(wait_src) + [(sDVE, sDVE.n)]
                    if first_te[0]:
                        w += [(sCON, CONSTS_LDW), (sDVE, CONSTS_DVE)]
                        first_te[0] = False
                r.op("tensor", lambda e, j=j, jj=jj, PSD=PSD: e.transpose(
                    PSD[:, jj * 128:(jj + 1) * 128],
                    src_tile[:, j * 128:(j + 1) * 128], idb[:]),
                    wait=w, inc=(sTE if jj == 7 else None))
            r.op("vector", lambda e, jg=jg, i=i, PSD=PSD: e.tensor_copy(
                dest_view_fn(jg, i),
                PSD[:, :].rearrange("p (j t) -> p j t", t=128)),
                wait=[(sTE, sTE.n)], inc=sDVE)

    def big8_dest(jg, i):
        v = BIG8[:, :].rearrange("p (j t) -> p j t", t=TOK)
        return v[:, jg * 8:(jg + 1) * 8, i * 128:i * 128 + 128]

    def run_pipelined(items, w_nat_pair, lhsT_pair):
        nw = len(w_nat_pair)
        tr_tok = {}
        evac_tok = {}

        def prep_dma(i):
            it = items[i]
            w = [(sTE, tr_tok[i - nw])] if i >= nw else [(sTE, sTE.n)]
            wn = w_nat_pair[i % nw]
            r.op("sync", lambda e, it=it, wn=wn: e.dma_start(
                out=wn[0:it["nrows"], :],
                in_=it["weight"][it["row"] * 128: it["row"] * 128 + it["nrows"], :]),
                wait=w, inc=sLDW, inc_amt=16)
            it["ldw"] = sLDW.n

        def prep_tr(i):
            it = items[i]
            nrows = it["nrows"]
            dst = lhsT_pair[i % 2]
            wn = w_nat_pair[i % nw]
            for kg in range((KCH + 3) // 4):
                nk = min(4, KCH - kg * 4)
                for kk in range(nk):
                    k = kg * 4 + kk
                    w = [(sLDW, it["ldw"])] if k == 0 else []
                    if kk == 0:
                        w.append((sDVE, sDVE.n))
                    half = kg % 2
                    r.op("tensor", lambda e, k=k, kk=kk, half=half, nrows=nrows, wn=wn:
                         e.transpose(PSC[0:128, half * 512 + kk * 128: half * 512 + (kk + 1) * 128],
                                     wn[0:nrows, k * 128:(k + 1) * 128], id32[:]),
                         wait=w, inc=(sTE if kk == nk - 1 else None))
                r.op("vector", lambda e, kg=kg, nk=nk, dst=dst: e.tensor_copy(
                    dst[0:128, kg * 512: kg * 512 + nk * 128],
                    PSC[:, (kg % 2) * 512: (kg % 2) * 512 + nk * 128]),
                    wait=[(sTE, sTE.n)], inc=sDVE)
            tr_tok[i] = sTE.n
            evac_tok[i] = sDVE.n

        prep_dma(0)
        prep_tr(0)
        for i, it in enumerate(items):
            if i + 1 < len(items):
                prep_dma(i + 1)
            it["body"](lhsT_pair[i % 2], evac_tok[i])
            if i + 1 < len(items):
                prep_tr(i + 1)
            it["post"](i % 2)
            if "extra" in it:
                it["extra"]()

    rope_first = [True]

    def rope(stage):
        w = [(sACT, sACT.n)] + [(sm, sm.n) for gg in range(6) for sm in sP[gg] if sm.n > 0]
        if rope_first[0]:
            w.append((sCON, CONSTS_LDW))
            rope_first[0] = False
        r.op("vector", lambda e, stage=stage: e.tensor_mul(
            RT1[0:64, :], stage[64:128, :], sinT[64:128, :]), wait=w, inc=sDVE)
        r.op("vector", lambda e, stage=stage: e.tensor_mul(
            RT1[64:128, :], stage[0:64, :], sinT[0:64, :]), inc=sDVE)
        r.op("vector", lambda e, stage=stage: e.tensor_mul(
            stage[:, :], stage[:, :], cosT[:, :]), wait=[(sDVE, sDVE.n)], inc=sDVE)
        r.op("vector", lambda e, stage=stage: e.tensor_add(
            stage[:, :], stage[:, :], RT1[:, :]), wait=[(sDVE, sDVE.n)], inc=sDVE)

    def pay_dma(fn, grp, par):
        sem = sP[grp][par]
        r.op("sync", fn, wait=[(sDVE, sDVE.n)], inc=sem, inc_amt=16)
        return (sem, sem.n)

    for i in range(TCH):
        buf = i % 2
        w = [(sDVE, sDVE.n), (sACT, sACT.n), (sPOOL, sPOOL.n)] if i >= 2 else []
        r.op("sync", lambda e, i=i, buf=buf: e.dma_start(
            out=xt[buf][:, :], in_=x_in[i * 128:(i + 1) * 128, :]),
            wait=w, inc=sLX[buf], inc_amt=16)
        ln_chain(xt[buf][:, :], xnb[buf][:, :], par=i,
                 extra_act=[(sLX[buf], sLX[buf].n)], extra_dve=[(sLX[buf], sLX[buf].n)],
                 extra_apply=[(sTE, sTE.n)])
        tm_to_fm(xnb[buf], big8_dest, i, wait_src=[(sPOOL, sPOOL.n)])

    psum_tok = {0: 0, 1: 0}
    stg_tok = {0: (sLDX, 0), 1: (sLDX, 0)}

    def proj_body(psum, ibuf):
        def body(lhsT_buf, evtok):
            for tb in range(NTB):
                for k in range(KCH):
                    w = []
                    if tb == 0 and k == 0:
                        w = [(sDVE, evtok), (sACT, psum_tok[ibuf])]
                    r.op("tensor", lambda e, tb=tb, k=k, lhsT_buf=lhsT_buf, psum=psum: e.matmul(
                        psum[:, tb * PTB:(tb + 1) * PTB],
                        lhsT_buf[:, k * 128:(k + 1) * 128],
                        BIG8[:, k * TOK + tb * PTB: k * TOK + (tb + 1) * PTB],
                        start=(k == 0), stop=(k == KCH - 1)),
                        wait=w, inc=(sTE if (k == KCH - 1 and tb == NTB - 1) else None))
        return body

    CC_Q = []
    CC_KV_box = [0]

    def k_post(g, ibuf):
        def post(buf):
            psum = PSA if ibuf == 0 else PSB
            r.op("scalar", lambda e: e.activation(STG[buf][:, :], psum[:, 0:TOK], AF.Copy,
                                                  bias=0.0, scale=1.0),
                 wait=[(sTE, sTE.n), stg_tok[ibuf]], inc=sACT)
            psum_tok[ibuf] = r.sems["sACT"].n
            rope(STG[buf])
            stg_tok[ibuf] = pay_dma(lambda e, g=g, buf=buf: e.dma_start(
                out=pay_kv[0:1, g * KVSH:(g + 1) * KVSH].rearrange("o (p t) -> p (o t)", p=128),
                in_=STG[buf][:, :]), 0, ibuf)
        return post

    def v_post(g, ibuf):
        def post(buf):
            psum = PSA if ibuf == 0 else PSB
            PSD = PSDh[0]
            r.op("scalar", lambda e: e.activation(STG[buf][:, :], psum[:, 0:TOK], AF.Copy,
                                                  bias=0.0, scale=1.0),
                 wait=[(sTE, sTE.n), stg_tok[ibuf]], inc=sACT)
            psum_tok[ibuf] = r.sems["sACT"].n
            for j in range(TCH):
                w = [(sACT, sACT.n), (sDVE, sDVE.n)] if j == 0 else []
                r.op("tensor", lambda e, j=j, buf=buf, PSD=PSD: e.transpose(
                    PSD[:, j * 128:(j + 1) * 128], STG[buf][:, j * 128:(j + 1) * 128], idb[:]),
                    wait=w, inc=(sTE if j == TCH - 1 else None))
            r.op("vector", lambda e, PSD=PSD: e.tensor_copy(VTM[:, :], PSD[:, 0:TOK]),
                 wait=[(sTE, sTE.n)] + [(sm, sm.n) for gg in range(6) for sm in sP[gg] if sm.n > 0],
                 inc=sDVE)
            stg_tok[ibuf] = pay_dma(lambda e, g=g: e.dma_start(
                out=pay_kv[0:1, 8 * KVSH + g * KVSH: 8 * KVSH + (g + 1) * KVSH]
                    .rearrange("o (j p cc) -> p o j cc", p=128, cc=128),
                in_=VTM[:, :].rearrange("p (j cc) -> p j cc", cc=128)), 0, ibuf)
            if g == 7:
                r.op("gpsimd", lambda e: e.collective_compute(
                    "AllGather", ALU.bypass, replica_groups=[list(range(8))],
                    ins=[pay_kv.opt()], outs=[gout_kv.opt()]),
                    wait=[(sP[0][0], sP[0][0].n), (sP[0][1], sP[0][1].n)], inc=sCC)
                CC_KV_box[0] = sCC.n
        return post

    def q_post(u, g, ibuf):
        def post(buf):
            psum = PSA if ibuf == 0 else PSB
            r.op("scalar", lambda e: e.activation(STG[buf][:, :], psum[:, 0:TOK], AF.Copy,
                                                  bias=0.0, scale=1.0),
                 wait=[(sTE, sTE.n), stg_tok[ibuf]], inc=sACT)
            psum_tok[ibuf] = r.sems["sACT"].n
            rope(STG[buf])
            stg_tok[ibuf] = pay_dma(lambda e, u=u, g=g, buf=buf: e.dma_start(
                out=pay_q[u][0:1, g * KVSH:(g + 1) * KVSH].rearrange("o (p t) -> p (o t)", p=128),
                in_=STG[buf][:, :]), 1 + u, buf)
            if g == 7:
                r.op("gpsimd", lambda e, u=u: e.collective_compute(
                    "AllGather", ALU.bypass, replica_groups=[list(range(8))],
                    ins=[pay_q[u].opt()], outs=[gout_q[u].opt()]),
                    wait=[(sP[1 + u][0], sP[1 + u][0].n), (sP[1 + u][1], sP[1 + u][1].n)], inc=sCC)
                CC_Q.append(sCC.n)
        return post

    items = []
    for g in range(8):
        i = len(items)
        items.append(dict(weight=Wk, row=g, nrows=128,
                          body=proj_body(PSA if i % 2 == 0 else PSB, i % 2), post=k_post(g, i % 2)))
        i = len(items)
        items.append(dict(weight=Wv, row=g, nrows=128,
                          body=proj_body(PSA if i % 2 == 0 else PSB, i % 2), post=v_post(g, i % 2)))
    for u in range(G):
        for g in range(8):
            i = len(items)
            items.append(dict(weight=Wq, row=g * G + u, nrows=128,
                              body=proj_body(PSA if i % 2 == 0 else PSB, i % 2),
                              post=q_post(u, g, i % 2)))
    import collections
    kv_done_box = [0]
    q0_done_box = [0]
    negm_tok = {}

    def k_extract_task(rr):
        def t():
            r.op("gpsimd", lambda e, rr=rr: e.indirect_dma_start(
                out=KTc[:, rr * TOK:(rr + 1) * TOK], out_offset=None,
                in_=gout_kv.rearrange("r (rw t) -> (r rw) t", t=TOK),
                in_offset=IOA(idxK[:, 0:1], 0),
                element_offset=rr * 16 * KVSH),
                wait=[(sCC, CC_KV_box[0])], inc=sKVX, inc_amt=16)
        return t

    def v_extract_task(rr, j):
        def t():
            r.op("gpsimd", lambda e, rr=rr, j=j: e.indirect_dma_start(
                out=Vc[:, (rr * TCH + j) * 128:(rr * TCH + j + 1) * 128], out_offset=None,
                in_=gout_kv.rearrange("r (rw cc) -> (r rw) cc", cc=128),
                in_offset=IOA(idxV[:, j:j + 1], 0),
                element_offset=rr * 16 * KVSH + 8 * KVSH),
                wait=[(sCC, CC_KV_box[0])], inc=sVX, inc_amt=16)
            kv_done_box[0] = r.sems["sVX"].n
        return t

    def q_extract_task(u, rr, qbuf):
        def t():
            r.op("gpsimd", lambda e, rr=rr, u=u, qbuf=qbuf: e.indirect_dma_start(
                out=qbuf[:, rr * TOK:(rr + 1) * TOK], out_offset=None,
                in_=gout_q[u].rearrange("r (rw t) -> (r rw) t", t=TOK),
                in_offset=IOA(idxK[:, 0:1], 0),
                element_offset=rr * 8 * KVSH),
                wait=[(sCC, CC_Q[u])], inc=sQX[u % 2], inc_amt=16)
            if rr == 7:
                q0_done_box[0] = r.sems["sQX0"].n
        return t

    def norm_steps(src_full, dst_scalar, pre_wait_fn):
        nseg = T // 512
        sq_tok = {}
        mm_tok = {}
        steps = []
        for sg in range(nseg):
            def ta(sg=sg):
                sseg = SQseg[sg % 2]
                w = list(pre_wait_fn()) if sg == 0 else []
                if sg >= 2:
                    w.append((sTE, mm_tok[sg - 2]))
                r.op("scalar", lambda e, sg=sg, sseg=sseg: e.activation(
                    sseg[:, :], src_full[:, sg * 512:(sg + 1) * 512], AF.Square,
                    bias=0.0, scale=1.0),
                    wait=w, inc=sACT)
                sq_tok[sg] = sACT.n

            def tb(sg=sg):
                sseg = SQseg[sg % 2]
                wm = [(sACT, sq_tok[sg])]
                if sg == 0:
                    wm.append((sDVE, sDVE.n))
                r.op("tensor", lambda e, sseg=sseg: e.matmul(
                    PSE[0:1, 0:512], onescol[:, 0:1], sseg[:, :],
                    start=True, stop=True),
                    wait=wm, inc=sTE)
                mm_tok[sg] = sTE.n
                r.op("scalar", lambda e: e.copy(segrow[:, :], PSE[0:1, 0:512]),
                     wait=[(sTE, mm_tok[sg]), (sDVE, sDVE.n)], inc=sACT)
                r.op("vector", lambda e: e.reduce_max(segmax[:, :], segrow[:, :], AXX),
                     wait=[(sACT, sACT.n), (sDVE, sDVE.n)], inc=sDVE)
                if sg == 0:
                    r.op("vector", lambda e: e.tensor_copy(dst_scalar, segmax[:, :]),
                         wait=[(sDVE, sDVE.n)], inc=sDVE)
                else:
                    r.op("vector", lambda e, dst_scalar=dst_scalar: e.tensor_max(
                        dst_scalar, dst_scalar, segmax[:, :]),
                        wait=[(sDVE, sDVE.n)], inc=sDVE)
            steps.append(ta)
            steps.append(tb)
        return steps

    def negm_chain(u):
        r.op("vector", lambda e: e.tensor_mul(mtmp[:, :], qn2m[:, :], kn2m[:, :]),
             wait=[(sDVE, sDVE.n)], inc=sDVE)
        r.op("scalar", lambda e: e.activation(mtmp2[:, :], mtmp[:, :], AF.Ln, bias=0.0,
                                              scale=1.0),
             wait=[(sDVE, sDVE.n)], inc=sACT)
        r.op("scalar", lambda e: e.activation(mtmp[:, :], mtmp2[:, :], AF.Exp, bias=0.0,
                                              scale=0.5),
             wait=[(sACT, sACT.n)], inc=sACT)
        r.op("vector", lambda e: e.tensor_scalar_mul(mtmp2[:, :], mtmp[:, :], -c.SCALE),
             wait=[(sACT, sACT.n)], inc=sDVE)
        r.op("vector", lambda e: e.tensor_copy(negmb[:, :], mtmp2[:, :]),
             wait=[(sDVE, sDVE.n)], inc=sDVE)
        r.op("tensor", lambda e: e.matmul(PSE[0:128, 0:1], onesb[0:1, :], negmb[0:1, 0:1],
                                          start=True, stop=True),
             wait=[(sDVE, sDVE.n), (sACT, sACT.n)], inc=sTE)
        r.op("scalar", lambda e, u=u: e.copy(negmcol[:, u:u + 1], PSE[0:128, 0:1]),
             wait=[(sTE, sTE.n)], inc=sACT)
        negm_tok[u] = sACT.n

    qA = collections.deque()
    for rr in range(8):
        qA.append(k_extract_task(rr))
    qA.extend(norm_steps(KTc[:, :], kn2m[:, :], lambda: [(sKVX, sKVX.n)]))
    for rr in range(8):
        for j in range(TCH):
            qA.append(v_extract_task(rr, j))
    qB = collections.deque()
    for rr in range(8):
        qB.append(q_extract_task(0, rr, Qb0))
    qB.extend(norm_steps(Qb0[:, :], qn2m[:, :], lambda: [(sQX[0], q0_done_box[0])]))
    qB.append(lambda: negm_chain(0))

    nitems = len(items)
    for i, it in enumerate(items):
        def extra(i=i, it=it):
            if i >= 16:
                budget = 4
                while qA and budget > 0:
                    qA.popleft()()
                    budget -= 1
            if i >= 24 and not qA:
                budget = 2
                while qB and budget > 0:
                    qB.popleft()()
                    budget -= 1
        it["extra"] = extra
    run_pipelined(items, [wnat], lhsT)
    CC_KV = CC_KV_box[0]
    while qA:
        qA.popleft()()
    while qB:
        qB.popleft()()
    KVEX = kv_done_box[0]

    r.barrier()
    p1.close()

    p2 = contextlib.ExitStack()
    a2 = lambda name, shape, dtp: p2.enter_context(nc.sbuf_tensor(name, shape, dtp))
    PSF = p2.enter_context(nc.psum_tensor("PSF", [128, 512], F32))
    Qb1 = a2("Qb1", [128, T], BF16)
    Qb = [Qb0, Qb1]
    OTb = [a2(f"OTb{i}", [128, T], BF16) for i in range(2)]
    PT = [a2(f"PT{i}", [128, 1024], BF16) for i in range(2)]
    recipb = a2("recipb", [128, 512], F32)

    prev_last_exp = [0]
    stt_tok = {0: 0, 1: 0}
    recip_tok = {0: 0, 1: 0}
    cc_o_tok = {}
    asm_tok = {}
    qex_tok = {0: q0_done_box[0]}
    scores_done = {-1: 0, 0: 0, 1: 0, 2: 0, 3: 0}

    def q_extract_mid(u, rr):
        ubuf2 = u % 2
        w = [(sCC, CC_Q[u])]
        if rr == 0 and u >= 2:
            w.append((sTE, scores_done[u - 2]))
        r.op("gpsimd", lambda e, rr=rr, u=u, ubuf2=ubuf2: e.indirect_dma_start(
            out=Qb[ubuf2][:, rr * TOK:(rr + 1) * TOK], out_offset=None,
            in_=gout_q[u].rearrange("r (rw t) -> (r rw) t", t=TOK),
            in_offset=IOA(idxK[:, 0:1], 0),
            element_offset=rr * 8 * KVSH),
            wait=w, inc=sQX[u % 2], inc_amt=16)

    oex_done = {}

    def o_extract(u):
        src = gassm if u == G - 1 else gout_o[u]
        wv = [(sCC, cc_o_tok[u])] if u < G - 1 else [(sASM, asm_tok[NP - 1])]
        for rr in range(8):
            r.op("gpsimd", lambda e, u=u, rr=rr, src=src: e.indirect_dma_start(
                out=BIG8[:, (rr * G + u) * TOK:(rr * G + u + 1) * TOK], out_offset=None,
                in_=src.rearrange("r (rw t) -> (r rw) t", t=TOK),
                in_offset=IOA(idxO[:, 0:1], 0),
                element_offset=rr * 128 * T),
                wait=(wv if rr == 0 else []), inc=sOX, inc_amt=16)
        oex_done[u] = sOX.n

    for u in range(G):
        ubuf = u % 2
        if u >= 1:
            for st in norm_steps(Qb[ubuf][:, :], qn2m[:, :],
                                 lambda: [(sQX[u % 2], qex_tok[u]), (sDVE, sDVE.n)]):
                st()
            negm_chain(u)
        QEX = qex_tok[u]
        NEGM_ACT = negm_tok[u]

        for bq in range(NB):
            lo = min(4, NB - 1)
            if u + 1 < G and lo <= bq < lo + 8 and NB >= lo + 8:
                q_extract_mid(u + 1, bq - lo)
                if bq == lo + 7:
                    qex_tok[u + 1] = sQX[(u + 1) % 2].n
            elif u + 1 < G and NB < lo + 8 and bq == 0:
                for rr2 in range(8):
                    q_extract_mid(u + 1, rr2)
                qex_tok[u + 1] = sQX[(u + 1) % 2].n
            nvis = DPB * bq + DPB
            npair = nvis // 2
            obank = bq % 2
            sbank = PSE if bq % 2 == 0 else PSF
            EXP_ACT = {}
            AVT = {}
            MSKS = {}

            def emit_avsums(pr, bq=bq, nvis=nvis, obank=obank, sbank=sbank,
                            EXP_ACT=EXP_ACT, AVT=AVT, MSKS=MSKS, QEX=QEX,
                            NEGM_ACT=NEGM_ACT):
                ptb = PT[pr % 2]
                for half in range(2):
                    sc = pr * 2 + half
                    w = [(sACT, EXP_ACT[pr])]
                    if sc in MSKS:
                        w.append((sPOOL, MSKS[sc]))
                    if sc == 0:
                        w.append((sDVE, stt_tok[obank]))
                        if bq == 0:
                            w.append((sVX, KVEX))
                    r.op("tensor", lambda e, sc=sc, half=half, ptb=ptb: e.matmul(
                        PSC[:, obank * 512: obank * 512 + 512],
                        Vc[:, sc * 128:(sc + 1) * 128],
                        ptb[:, half * 512:(half + 1) * 512],
                        start=(sc == 0), stop=(sc == nvis - 1)),
                        wait=w)
                    w2 = []
                    if sc == 0:
                        w2 = [(sDVE, recip_tok[bq % 2])]
                        if bq == 0:
                            w2.append((sACT, NEGM_ACT))
                    r.op("tensor", lambda e, sc=sc, half=half, ptb=ptb: e.matmul(
                        sbank[:, 0:512], onesS[:, :],
                        ptb[:, half * 512:(half + 1) * 512],
                        start=(sc == 0), stop=(sc == nvis - 1)),
                        wait=w2, inc=(sTE if half == 1 else None))
                AVT[pr] = sTE.n

            pending = None
            for pr in range(npair):
                ps = PSA if pr % 2 == 0 else PSB
                for half in range(2):
                    sc = pr * 2 + half
                    w = []
                    if half == 0:
                        if pr >= 2:
                            w.append((sACT, EXP_ACT[pr - 2]))
                        else:
                            w.append((sACT, prev_last_exp[0]))
                        if pr == 0 and bq == 0:
                            w += [(sQX[u % 2], QEX), (sACT, NEGM_ACT)]
                    r.op("tensor", lambda e, sc=sc, half=half, ps=ps, bq=bq, ubuf=ubuf: e.matmul(
                        ps[:, half * 512:(half + 1) * 512],
                        KTc[:, sc * 128:(sc + 1) * 128],
                        Qb[ubuf][:, bq * QB: bq * QB + 512],
                        start=True, stop=True),
                        wait=w, inc=(sTE if half == 1 else None))
                SC_TE = sTE.n
                ptb = PT[pr % 2]
                w = [(sTE, SC_TE)]
                if pr >= 2:
                    w.append((sTE, AVT[pr - 2]))
                if pr == 0 and bq == 0:
                    w.append((sACT, NEGM_ACT))
                r.op("scalar", lambda e, ps=ps, ptb=ptb, u=u: e.activation(
                    ptb[:, :], ps[:, :], AF.Exp, bias=negmcol[:, u:u + 1], scale=c.SCALE),
                    wait=w, inc=sACT)
                EXP_ACT[pr] = sACT.n
                for half in range(2):
                    sc = pr * 2 + half
                    m = sc - DPB * bq
                    if m >= 0:
                        if m > 0:
                            r.op("gpsimd", lambda e, ptb=ptb, half=half, m=m: e.memset(
                                ptb[:, half * 512: half * 512 + m * 128], 0.0),
                                wait=[(sACT, EXP_ACT[pr])])
                        r.op("gpsimd", lambda e, ptb=ptb, half=half, m=m: e.tensor_mul(
                            ptb[:, half * 512 + m * 128: half * 512 + (m + 1) * 128],
                            ptb[:, half * 512 + m * 128: half * 512 + (m + 1) * 128],
                            tri[:, :]),
                            wait=[(sACT, EXP_ACT[pr])], inc=sPOOL)
                        MSKS[sc] = sPOOL.n
                if pending is not None:
                    emit_avsums(pending)
                pending = pr
            emit_avsums(pending)
            prev_last_exp[0] = EXP_ACT[npair - 1]
            r.op("vector", lambda e, sbank=sbank: e.reciprocal(recipb[:, :], sbank[:, 0:512]),
                 wait=[(sTE, AVT[npair - 1]), (sDVE, sDVE.n)], inc=sDVE)
            recip_tok[bq % 2] = sDVE.n
            wstt = [(sDVE, sDVE.n)]
            if bq == 0:
                wstt.append((sP[5][ubuf], sP[5][ubuf].n))
            r.op("vector", lambda e, ubuf=ubuf, bq=bq, obank=obank: e.scalar_tensor_tensor(
                OTb[ubuf][:, bq * QB: bq * QB + 512], PSC[:, obank * 512: obank * 512 + 512],
                SUMSCALE, recipb[:, :], ALU.mult, ALU.mult),
                wait=wstt, inc=sDVE)
            stt_tok[obank] = sDVE.n
            if u == G - 1 and (bq + 1) % BPP == 0:
                p = bq // BPP
                psem = sP[5][p % 2]
                r.op("sync", lambda e, ubuf=ubuf, p=p, psem=psem: e.dma_start(
                    out=pay_ol[p][0:1, :].rearrange("o (pp t) -> pp (o t)", pp=128),
                    in_=OTb[ubuf][:, p * TPP:(p + 1) * TPP]),
                    wait=[(sDVE, sDVE.n)], inc=psem, inc_amt=16)
                r.op("gpsimd", lambda e, p=p, psem=psem: e.collective_compute(
                    "AllGather", ALU.bypass, replica_groups=[list(range(8))],
                    ins=[pay_ol[p].opt()], outs=[gout_ol[p].opt()]),
                    wait=[(psem, psem.n)], inc=sCC)
                cctok = r.sems["sCC"].n
                r.op("sync", lambda e, p=p: e.dma_start(
                    out=gassm.rearrange("r (rw pp t) -> r rw pp t", pp=NP, t=TPP)[:, :, p, :],
                    in_=gout_ol[p].rearrange("r (rw t) -> r rw t", t=TPP)),
                    wait=[(sCC, cctok)], inc=sASM, inc_amt=16)
                asm_tok[p] = r.sems["sASM"].n
        if u < G - 1:
            psem = sP[5][u % 2]
            r.op("sync", lambda e, u=u, ubuf=ubuf: e.dma_start(
                out=pay_o[u][0:1, :].rearrange("o (pp t) -> pp (o t)", pp=128),
                in_=OTb[ubuf][:, :]),
                wait=[(sDVE, sDVE.n)], inc=psem, inc_amt=16)
            r.op("gpsimd", lambda e, u=u, psem=psem: e.collective_compute(
                "AllGather", ALU.bypass, replica_groups=[list(range(8))],
                ins=[pay_o[u].opt()], outs=[gout_o[u].opt()]),
                wait=[(psem, psem.n)], inc=sCC)
            cc_o_tok[u] = sCC.n
        scores_done[u] = r.sems["sTE"].n
        if u >= 1:
            o_extract(u - 1)
    o_extract(G - 1)
    OEX = sOX.n

    r.barrier()
    p2.close()
    p12.close()

    p3o = contextlib.ExitStack()
    BIGB = p3o.enter_context(nc.sbuf_tensor("BIGB", [128, TCH * D], BF16))
    PSD3 = p3o.enter_context(nc.psum_tensor("PSD3", [128, 1024], BF16))
    PSDh[0] = PSD3
    p3 = contextlib.ExitStack()
    a3 = lambda name, shape, dtp: p3.enter_context(nc.sbuf_tensor(name, shape, dtp))
    wnat3 = [a3("wnat3", [128, D], F32)]
    lhsT3 = [a3(f"lhsT3{i}", [128, KCH * 128], BF16) for i in range(2)]
    YST = [a3(f"YST{i}", [128, TOK], BF16) for i in range(2)]
    xt3 = a3("xt3", [128, D], F32)
    xnb3 = a3("xnb3", [128, D], BF16)

    def dest_tm(dest_big, m):
        v = dest_big[:, :].rearrange("p (j d) -> p j d", d=D)
        return v[:, :, m * 128:(m + 1) * 128]

    fm_psum_tok = {0: 0, 1: 0}

    def fm_items(weight_ap, stg_pair, dest_big, bias_col=None, extra_mm=None,
                 first_wait=(), first_evac_wait=()):
        its = []
        for m in range(KCH):
            def body(lhsT_buf, evtok, m=m):
                psum = PSA if m % 2 == 0 else PSB
                for tb in range(NTB):
                    for k in range(KCH):
                        w = []
                        if tb == 0 and k == 0:
                            w = [(sDVE, evtok), (sACT, fm_psum_tok[m % 2])]
                            if m == 0:
                                w += list(first_wait)
                        last = (k == KCH - 1) and (extra_mm is None)
                        r.op("tensor", lambda e, tb=tb, k=k, lhsT_buf=lhsT_buf, last=last, psum=psum: e.matmul(
                            psum[:, tb * PTB:(tb + 1) * PTB],
                            lhsT_buf[:, k * 128:(k + 1) * 128],
                            BIG8[:, k * TOK + tb * PTB: k * TOK + (tb + 1) * PTB],
                            start=(k == 0), stop=last),
                            wait=w,
                            inc=(sTE if (last and tb == NTB - 1) else None))
                    if extra_mm is not None:
                        extra_mm(m, tb, psum)

            def post(buf, m=m):
                psum = PSA if m % 2 == 0 else PSB
                PSD = PSDh[0]
                sbuf = m % 2
                if bias_col is None:
                    r.op("scalar", lambda e, psum=psum, sbuf=sbuf: e.activation(
                        stg_pair[sbuf][:, :], psum[:, 0:TOK], AF.Copy, bias=0.0, scale=1.0),
                        wait=[(sTE, sTE.n)], inc=sACT)
                else:
                    r.op("scalar", lambda e, psum=psum, sbuf=sbuf, m=m: e.activation(
                        stg_pair[sbuf][:, :], psum[:, 0:TOK], AF.Identity,
                        bias=bias_col[:, m:m + 1], scale=1.0),
                        wait=[(sTE, sTE.n), (sCON, CONSTS_LDW)], inc=sACT)
                fm_psum_tok[m % 2] = r.sems["sACT"].n
                for j in range(TCH):
                    w = [(sACT, sACT.n), (sDVE, sDVE.n)] if j == 0 else []
                    r.op("tensor", lambda e, j=j, sbuf=sbuf, PSD=PSD: e.transpose(
                        PSD[:, j * 128:(j + 1) * 128],
                        stg_pair[sbuf][:, j * 128:(j + 1) * 128], idb[:]),
                        wait=w, inc=(sTE if j == TCH - 1 else None))
                w = [(sTE, sTE.n)]
                if m == 0:
                    w += list(first_evac_wait)
                r.op("vector", lambda e, m=m, PSD=PSD: e.tensor_copy(
                    dest_tm(dest_big, m), PSD[:, 0:TOK].rearrange("p (j t) -> p j t", t=128)),
                    wait=w, inc=sDVE)
            its.append(dict(weight=weight_ap, row=m, nrows=128, body=body, post=post))
        return its

    if DEBUG[0] & 4:
        r.op("gpsimd", lambda e: e.dma_start(out=dbg_att[:, :], in_=BIG8[:, :]),
             wait=[(sOX, OEX)], inc=sST, inc_amt=16)
    run_pipelined(fm_items(Wo, YST, BIGB, first_wait=[(sOX, OEX)]), wnat3, lhsT3)

    for i in range(TCH):
        r.op("sync", lambda e, i=i: e.dma_start(out=xt3[:, :],
                                                in_=x_in[i * 128:(i + 1) * 128, :]),
             wait=[(sDVE, sDVE.n), (sPOOL, sPOOL.n)], inc=sLDX, inc_amt=16)
        src = BIGB[:, i * D:(i + 1) * D]
        ln_chain(src, xnb3[:, :], par=i,
                 extra_act=[(sDVE, sDVE.n)], extra_dve=[(sDVE, sDVE.n)])
        r.op("gpsimd", lambda e, i=i: e.tensor_add(
            BIGB[:, i * D:(i + 1) * D], xt3[:, :], xnb3[:, :]),
            wait=[(sPOOL, sPOOL.n), (sLDX, sLDX.n)], inc=sPOOL)
        r.op("gpsimd", lambda e, i=i: e.dma_start(
            out=z1d[i * 128:(i + 1) * 128, :], in_=BIGB[:, i * D:(i + 1) * D]),
            wait=[(sPOOL, sPOOL.n)], inc=sZ1, inc_amt=16)
    Z1ST = sZ1.n

    r.barrier()
    p3.close()

    p4 = contextlib.ExitStack()
    a4 = lambda name, shape, dtp: p4.enter_context(nc.sbuf_tensor(name, shape, dtp))
    wnat4 = [a4("wnat4", [128, D], F32), ]
    lhsT4 = [a4(f"lhsT4{i}", [128, KCH * 128], BF16) for i in range(2)]
    HST0 = a4("HST0", [128, TOK], BF16)
    HST = [HST0, HST0]
    xnb4 = a4("xnb4", [128, D], BF16)
    AgT = a4("AgT", [128, KCH * 68], BF16)
    BTt = a4("BTt", [64, D], BF16)
    ug = a4("ug", [68, TOK], F32)
    umask = a4("umask", [64, TOK], BF16)
    gt = a4("gt", [128, 4 * TCH], F32)
    smaxc = a4("smaxc", [128, TCH], F32)
    indt = a4("indt", [128, 4 * TCH], BF16)
    indf = a4("indf", [4, TOK], BF16)

    r.op("sync", lambda e: e.dma_start(out=wnat4[0][0:64, :], in_=Af[:, :]),
         wait=[(sTE, sTE.n)], inc=sLDW, inc_amt=16)
    r.op("sync", lambda e: e.dma_start(out=wnat4[0][64:68, :], in_=gw[:, :]),
         inc=sLDW, inc_amt=16)
    for k in range(KCH):
        kk = k % 4
        half = (k // 4) % 2
        w = [(sLDW, sLDW.n)] if k == 0 else []
        if kk == 0:
            w.append((sDVE, sDVE.n))
        r.op("tensor", lambda e, k=k, kk=kk, half=half: e.transpose(
            PSC[0:128, half * 512 + kk * 128: half * 512 + kk * 128 + 68],
            wnat4[0][0:68, k * 128:(k + 1) * 128], id32[0:68, 0:68]),
            wait=w, inc=(sTE if kk == 3 or k == KCH - 1 else None))
        if kk == 3 or k == KCH - 1:
            kg = k // 4
            nk = kk + 1
            r.op("vector", lambda e, kg=kg, half=half, nk=nk: e.tensor_copy(
                AgT[0:128, :].rearrange("p (kx q) -> p kx q", q=68)[:, kg * 4: kg * 4 + nk, :],
                PSC[:, half * 512: half * 512 + nk * 128]
                .rearrange("p (kx q) -> p kx q", q=128)[:, :, 0:68]),
                wait=[(sTE, sTE.n)], inc=sDVE)
    r.op("sync", lambda e: e.dma_start(
        out=wnat4[0][0:128, 0:64 * KCH].rearrange("p (m q) -> p m q", q=64),
        in_=Bf.rearrange("(m p) q -> p m q", p=128)),
        wait=[(sTE, sTE.n), (sDVE, sDVE.n)], inc=sLDW, inc_amt=16)
    for m in range(KCH):
        mm4 = m % 4
        half = (m // 4) % 2
        w = [(sLDW, sLDW.n)] if m == 0 else []
        if mm4 == 0:
            w.append((sDVE, sDVE.n))
        r.op("tensor", lambda e, m=m, mm4=mm4, half=half: e.transpose(
            PSC[0:64, half * 512 + mm4 * 128: half * 512 + (mm4 + 1) * 128],
            wnat4[0][:, m * 64:(m + 1) * 64], id32[:]),
            wait=w, inc=(sTE if mm4 == 3 or m == KCH - 1 else None))
        if mm4 == 3 or m == KCH - 1:
            mg = m // 4
            nm = mm4 + 1
            r.op("vector", lambda e, mg=mg, half=half, nm=nm: e.tensor_copy(
                BTt[0:64, mg * 512: mg * 512 + nm * 128],
                PSC[0:64, half * 512: half * 512 + nm * 128]),
                wait=[(sTE, sTE.n)], inc=sDVE)

    for i in range(TCH):
        src = BIGB[:, i * D:(i + 1) * D]
        ln_chain(src, xnb4[:, :], par=i,
                 extra_act=[(sPOOL, sPOOL.n)], extra_dve=[(sPOOL, sPOOL.n)],
                 extra_apply=[(sTE, sTE.n)])
        tm_to_fm(xnb4, big8_dest, i, wait_src=[(sPOOL, sPOOL.n)])

    if DEBUG[0] & 1:
        r.op("gpsimd", lambda e: e.dma_start(out=dbg_z1[:, :], in_=z1d[:, :]),
             wait=[(sZ1, Z1ST)], inc=sST, inc_amt=16)
    if DEBUG[0] & 2:
        r.op("gpsimd", lambda e: e.dma_start(out=dbg_znt[:, :], in_=BIG8[:, :]),
             wait=[(sTE, sTE.n), (sDVE, sDVE.n)], inc=sST, inc_amt=16)

    for tb in range(NTB):
        for k in range(KCH):
            w = []
            if tb == 0 and k == 0:
                w = [(sDVE, sDVE.n), (sACT, sACT.n)]
            r.op("tensor", lambda e, tb=tb, k=k: e.matmul(
                PSA[0:68, tb * PTB:(tb + 1) * PTB],
                AgT[:, k * 68:(k + 1) * 68],
                BIG8[:, k * TOK + tb * PTB: k * TOK + (tb + 1) * PTB],
                start=(k == 0), stop=(k == KCH - 1)),
                wait=w, inc=(sTE if (k == KCH - 1 and tb == NTB - 1) else None))
    r.op("vector", lambda e: e.tensor_scalar(ug[:, :], PSA[0:68, 0:TOK],
                                             gbias[0:68, 0:1], None, ALU.add),
         wait=[(sTE, sTE.n), (sCON, CONSTS_LDW)], inc=sDVE)
    for i in range(TCH):
        w = [(sDVE, sDVE.n)] if i == 0 else []
        r.op("tensor", lambda e, i=i: e.transpose(
            PSB[:, i * 4:(i + 1) * 4], ug[64:68, i * 128:(i + 1) * 128], id32[64:68, 64:68]),
            wait=w, inc=(sTE if i == TCH - 1 else None))
    r.op("scalar", lambda e: e.copy(gt[:, 0:4 * TCH], PSB[:, 0:4 * TCH]),
         wait=[(sTE, sTE.n)], inc=sACT)
    r.op("vector", lambda e: e.reduce_max(
        smaxc[:, :], gt[:, :].rearrange("p (i g) -> p i g", g=4), AXX),
        wait=[(sACT, sACT.n)], inc=sDVE)
    for i in range(TCH):
        r.op("vector", lambda e, i=i: e.tensor_single_scalar(
            indt[:, i * 4:(i + 1) * 4], gt[:, i * 4:(i + 1) * 4],
            smaxc[:, i:i + 1], ALU.is_ge),
            wait=[(sDVE, sDVE.n)], inc=sDVE)
    for i in range(TCH):
        w = [(sDVE, sDVE.n)] if i == 0 else []
        r.op("tensor", lambda e, i=i: e.transpose(
            PSDh[0][0:4, i * 128:(i + 1) * 128], indt[:, i * 4:(i + 1) * 4], idb[:]),
            wait=w, inc=(sTE if i == TCH - 1 else None))
    r.op("vector", lambda e: e.tensor_copy(indf[0:4, :], PSDh[0][0:4, 0:TOK]),
         wait=[(sTE, sTE.n)], inc=sDVE)
    for tb in range(NTB):
        r.op("tensor", lambda e, tb=tb: e.matmul(
            PSE[0:64, 0:PTB], mk[0:4, :], indf[0:4, tb * PTB:(tb + 1) * PTB],
            start=True, stop=True),
            wait=[(sDVE, sDVE.n)], inc=sTE)
        r.op("vector", lambda e, tb=tb: e.scalar_tensor_tensor(
            umask[0:64, tb * PTB:(tb + 1) * PTB], PSE[0:64, 0:PTB], 1.0,
            ug[0:64, tb * PTB:(tb + 1) * PTB], ALU.min, ALU.mult),
            wait=[(sTE, sTE.n)], inc=sDVE)
    UMASK_DVE = sDVE.n
    if DEBUG[0] & 32:
        r.op("gpsimd", lambda e: e.dma_start(out=dbg_gt[:, :], in_=gt[:, :]),
             wait=[(sDVE, UMASK_DVE)], inc=sST, inc_amt=16)
    if DEBUG[0] & 64:
        r.op("gpsimd", lambda e: e.dma_start(out=dbg_if[:, :], in_=indf[0:4, :]),
             wait=[(sDVE, UMASK_DVE)], inc=sST, inc_amt=16)
    if DEBUG[0] & 128:
        r.op("gpsimd", lambda e: e.dma_start(out=dbg_ug[:, :], in_=ug[:, :]),
             wait=[(sDVE, UMASK_DVE)], inc=sST, inc_amt=16)
    if DEBUG[0] & 16:
        r.op("gpsimd", lambda e: e.dma_start(out=dbg_um[:, :], in_=umask[0:64, :]),
             wait=[(sDVE, UMASK_DVE)], inc=sST, inc_amt=16)


    def lora_mm(m, tb, psum):
        w = [(sDVE, UMASK_DVE)] if (m == 0 and tb == 0) else []
        r.op("tensor", lambda e, m=m, tb=tb, psum=psum: e.matmul(
            psum[:, tb * PTB:(tb + 1) * PTB],
            BTt[0:64, m * 128:(m + 1) * 128],
            umask[0:64, tb * PTB:(tb + 1) * PTB],
            start=False, stop=True),
            wait=w, inc=(sTE if tb == NTB - 1 else None))

    fm_psum_tok[0] = r.sems["sACT"].n
    fm_psum_tok[1] = r.sems["sACT"].n
    run_pipelined(fm_items(Wf, HST, BIGB, bias_col=bcolv, extra_mm=lora_mm,
                           first_evac_wait=[(sZ1, Z1ST), (sPOOL, r.sems["sPOOL"].n)]),
                  wnat4, lhsT4)

    r.barrier()
    p4.close()

    if DEBUG[0] & 8:
        r.op("gpsimd", lambda e: e.dma_start(out=dbg_htm[:, :], in_=BIGB[:, :]),
             inc=sST, inc_amt=16)
    p5 = contextlib.ExitStack()
    a5 = lambda name, shape, dtp: p5.enter_context(nc.sbuf_tensor(name, shape, dtp))
    xt5 = [a5(f"xt5{i}", [128, D], BF16) for i in range(2)]
    xnb5 = a5("xnb5", [128, D], BF16)
    zout = [a5(f"zout{i}", [128, D], F32) for i in range(2)]

    for i in range(TCH):
        bufi = i % 2
        w = [(sPOOL, sPOOL.n)] if i >= 2 else []
        if i == 0:
            w = [(sZ1, Z1ST)]
        r.op("sync", lambda e, i=i, bufi=bufi: e.dma_start(
            out=xt5[bufi][:, :], in_=z1d[i * 128:(i + 1) * 128, :]),
            wait=w, inc=sLX[bufi], inc_amt=16)
        src = BIGB[:, i * D:(i + 1) * D]
        ln_chain(src, xnb5[:, :], par=i,
                 extra_act=[(sDVE, sDVE.n)], extra_dve=[(sDVE, sDVE.n)])
        w = [(sPOOL, sPOOL.n), (sLX[bufi], sLX[bufi].n)]
        if i >= 2:
            w.append((sST, sST.n))
        r.op("gpsimd", lambda e, bufi=bufi: e.tensor_add(
            zout[bufi][:, :], xt5[bufi][:, :], xnb5[:, :]),
            wait=w, inc=sPOOL)
        r.op("gpsimd", lambda e, i=i, bufi=bufi: e.dma_start(
            out=out_d[i * 128:(i + 1) * 128, :], in_=zout[bufi][:, :]),
            wait=[(sPOOL, sPOOL.n)], inc=sST, inc_amt=16)
    r.waitonly("gpsimd", [(sST, sST.n)])

    with nc.Block() as block:
        block.sync(lambda e: [f(e) for f in r.q["sync"]])
        block.scalar(lambda e: [f(e) for f in r.q["scalar"]])
        block.vector(lambda e: [f(e) for f in r.q["vector"]])
        block.tensor(lambda e: [f(e) for f in r.q["tensor"]])
        block.gpsimd(lambda e: [f(e) for f in r.q["gpsimd"]])

    return nc


def make_inputs(cfg, core, full):
    c = cfg
    TOK, D, TCH = c.TOK, c.D, c.TCH
    x = np.ascontiguousarray(full["x"].reshape(c.T, D)[core * TOK:(core + 1) * TOK])
    cos, sin = full["cos"], full["sin"]
    cosT = np.ascontiguousarray(cos[core * TOK:(core + 1) * TOK].T)
    sinT = sin[core * TOK:(core + 1) * TOK].T.copy()
    sinT[64:128, :] = -sinT[64:128, :]
    ii = np.arange(128, dtype=np.int32)
    idxK = (core * 128 + ii).reshape(128, 1)
    idxO = (8 * ii + core).reshape(128, 1)
    idxV = np.stack([core * TOK + j * 128 + ii for j in range(TCH)], axis=1).astype(np.int32)
    tri = np.triu(np.ones((128, 128), dtype=np.float32))
    mkm = np.zeros((4, 64), dtype=np.float32)
    for g, rk in enumerate(c.RANKS):
        mkm[g, :rk] = 1.0
    return {
        "x_in": x,
        "Wq": full["Wq"], "Wk": full["Wk"], "Wv": full["Wv"], "Wo": full["Wo"],
        "Wf": full["W"], "bf_in": full["b"], "Af": full["A"], "Bf": full["B"],
        "gw": full["gate_w"], "gb": full["gate_b"],
        "cosd": cosT, "sind": np.ascontiguousarray(sinT),
        "id32d": np.eye(128, dtype=np.float32),
        "idbd": np.eye(128, dtype=np.float32).astype(BFNP),
        "trid": tri.astype(BFNP),
        "mkd": mkm.astype(BFNP),
        "idxKd": idxK, "idxOd": idxO, "idxVd": np.ascontiguousarray(idxV),
    }


def run(cfg, full, trace=False):
    nc = build_program(cfg)
    in_maps = [make_inputs(cfg, corei, full) for corei in range(8)]
    res = run_bass_kernel_spmd(nc, in_maps, core_ids=list(range(8)), trace=trace)
    outs = [res.results[i]["out_d"] for i in range(8)]
    if DEBUG[0]:
        ks = [k for b, k in ((1, "dbg_z1"), (2, "dbg_znt"), (4, "dbg_att"),
                             (8, "dbg_htm"), (16, "dbg_um"), (32, "dbg_gt"),
                             (64, "dbg_if"), (128, "dbg_ug")) if DEBUG[0] & b]
        dbg = {k: [res.results[i][k] for i in range(8)] for k in ks}
        return np.concatenate(outs, axis=0), (res, dbg)
    return np.concatenate(outs, axis=0), res


def kernel(L0_attn_x, L0_attn_Wq, L0_attn_Wk, L0_attn_Wv, L0_attn_Wo,
           L0_ffn_W, L0_ffn_b, L0_ffn_A, L0_ffn_B, L0_ffn_gate_w, L0_ffn_gate_b):
    cfg = REAL
    cos, sin = rope_tables(cfg.T)
    full = {
        "x": np.asarray(L0_attn_x), "Wq": np.asarray(L0_attn_Wq),
        "Wk": np.asarray(L0_attn_Wk), "Wv": np.asarray(L0_attn_Wv),
        "Wo": np.asarray(L0_attn_Wo), "W": np.asarray(L0_ffn_W),
        "b": np.asarray(L0_ffn_b), "A": np.asarray(L0_ffn_A),
        "B": np.asarray(L0_ffn_B), "gate_w": np.asarray(L0_ffn_gate_w),
        "gate_b": np.asarray(L0_ffn_gate_b),
        "cos": cos, "sin": sin,
    }
    out, _ = run(cfg, full)
    return out.reshape(1, cfg.T, cfg.D)


# version 13 (final)
# speedup vs baseline: 1.1280x; 1.1280x over previous
import math
import contextlib
import numpy as np
import ml_dtypes
import concourse.bass as bass
import concourse.mybir as mybir
from concourse.bass_utils import run_bass_kernel_spmd

F32 = mybir.dt.float32
BF16 = mybir.dt.bfloat16
I32 = mybir.dt.int32
AF = mybir.ActivationFunctionType
ALU = mybir.AluOpType
AXX = mybir.AxisListType.X
IOA = bass.IndirectOffsetOnAxis

BFNP = ml_dtypes.bfloat16
SUMSCALE = float(2 ** 26)
DEBUG = [False]


class Cfg:
    def __init__(self, TOK, D, NH):
        self.NC = 8
        self.TOK = TOK
        self.T = 8 * TOK
        self.D = D
        self.NH = NH
        self.NKV = 8
        self.HD = 128
        self.G = NH // 8
        self.KCH = D // 128
        self.TCH = TOK // 128
        self.QB = 512
        self.NB = self.T // self.QB
        self.DPB = self.QB // 128
        self.PTB = min(512, TOK)
        self.NTB = max(1, TOK // 512)
        self.RANKS = (8, 16, 32, 64)
        self.EPS = 1e-5
        self.SCALE = 1.0 / math.sqrt(self.HD)
        assert D == NH * 128
        assert TOK % 128 == 0 and self.T % self.QB == 0


REAL = Cfg(TOK=1024, D=4096, NH=32)


def rope_tables(T):
    HD = 128
    theta, factor, high_ff, low_ff, orig_max = 500000.0, 8.0, 4.0, 1.0, 8192
    inv = 1.0 / theta ** (np.arange(0, HD, 2, dtype=np.float32) / HD)
    wavelen = 2.0 * np.pi / inv
    low_wl = orig_max / low_ff
    high_wl = orig_max / high_ff
    inv_scaled = np.where(wavelen > low_wl, inv / factor, inv)
    smooth = (orig_max / wavelen - low_ff) / (high_ff - low_ff)
    smoothed = (1.0 - smooth) / factor * inv + smooth * inv
    is_medium = (wavelen <= low_wl) & (wavelen >= high_wl)
    invf = np.where(is_medium, smoothed, inv_scaled).astype(np.float32)
    pos = np.arange(T, dtype=np.float32)
    freqs = pos[:, None] * invf[None, :]
    emb = np.concatenate([freqs, freqs], axis=-1)
    return np.cos(emb).astype(np.float32), np.sin(emb).astype(np.float32)


class Sem:
    __slots__ = ("h", "n")

    def __init__(self, h):
        self.h = h
        self.n = 0


class Rec:
    ENGS = ("sync", "scalar", "vector", "tensor", "gpsimd")

    def __init__(self, nc):
        self.nc = nc
        self.q = {e: [] for e in self.ENGS}
        self._stack = contextlib.ExitStack()
        self.sems = {}

    def sem(self, name):
        if name not in self.sems:
            self.sems[name] = Sem(self._stack.enter_context(self.nc.semaphore(name)))
        return self.sems[name]

    def op(self, eng, fn, wait=(), inc=None, inc_amt=1):
        wait = [(s, v) for (s, v) in wait if v > 0]
        for s, v in wait:
            assert v <= s.n, f"wait on unrecorded value {v} > {s.n}"
        if inc is not None:
            inc.n += inc_amt

        def run(e, wait=wait, inc=inc, inc_amt=inc_amt, fn=fn):
            for s, v in wait:
                e.wait_ge(s.h, v)
            inst = fn(e)
            if inc is not None:
                inst.then_inc(inc.h, inc_amt)
        self.q[eng].append(run)

    def waitonly(self, eng, wait):
        wait = [(s, v) for (s, v) in wait if v > 0]
        for s, v in wait:
            assert v <= s.n

        def run(e, wait=wait):
            for s, v in wait:
                e.wait_ge(s.h, v)
        self.q[eng].append(run)

    def barrier(self):
        s = self.sem("sPH")
        base = s.n
        for eng in self.ENGS:
            self.op(eng, lambda e, s=s: e.sem_inc(s.h, 1))
            s.n += 1
        for eng in self.ENGS:
            self.waitonly(eng, [(s, base + 5)])


def build_program(cfg):
    nc = bass.Bass("TRN2", target_bir_lowering=False)
    c = cfg
    TOK, D, KCH, TCH, T = c.TOK, c.D, c.KCH, c.TCH, c.T
    NB, QB, DPB, G = c.NB, c.QB, c.DPB, c.G
    PTB, NTB = c.PTB, c.NTB
    KVSH = 128 * TOK

    dt = nc.dram_tensor
    x_in = dt("x_in", [TOK, D], F32, kind="ExternalInput").ap()
    Wq = dt("Wq", [D, D], F32, kind="ExternalInput").ap()
    Wk = dt("Wk", [1024, D], F32, kind="ExternalInput").ap()
    Wv = dt("Wv", [1024, D], F32, kind="ExternalInput").ap()
    Wo = dt("Wo", [D, D], F32, kind="ExternalInput").ap()
    Wf = dt("Wf", [D, D], F32, kind="ExternalInput").ap()
    bf_in = dt("bf_in", [D], F32, kind="ExternalInput").ap()
    Af = dt("Af", [64, D], F32, kind="ExternalInput").ap()
    Bf = dt("Bf", [D, 64], F32, kind="ExternalInput").ap()
    gw = dt("gw", [4, D], F32, kind="ExternalInput").ap()
    gb = dt("gb", [4], F32, kind="ExternalInput").ap()
    cosd = dt("cosd", [128, TOK], F32, kind="ExternalInput").ap()
    sind = dt("sind", [128, TOK], F32, kind="ExternalInput").ap()
    id32d = dt("id32d", [128, 128], F32, kind="ExternalInput").ap()
    idbd = dt("idbd", [128, 128], BF16, kind="ExternalInput").ap()
    trid = dt("trid", [128, 128], BF16, kind="ExternalInput").ap()
    mkd = dt("mkd", [4, 64], BF16, kind="ExternalInput").ap()
    idxKd = dt("idxKd", [128, 1], I32, kind="ExternalInput").ap()
    idxOd = dt("idxOd", [128, 1], I32, kind="ExternalInput").ap()
    idxVd = dt("idxVd", [128, TCH], I32, kind="ExternalInput").ap()
    out_d = dt("out_d", [TOK, D], F32, kind="ExternalOutput").ap()
    if DEBUG[0] & 1:
        dbg_z1 = dt("dbg_z1", [TOK, D], BF16, kind="ExternalOutput").ap()
    if DEBUG[0] & 2:
        dbg_znt = dt("dbg_znt", [128, KCH * TOK], BF16, kind="ExternalOutput").ap()
    if DEBUG[0] & 4:
        dbg_att = dt("dbg_att", [128, KCH * TOK], BF16, kind="ExternalOutput").ap()
    if DEBUG[0] & 8:
        dbg_htm = dt("dbg_htm", [128, TCH * D], BF16, kind="ExternalOutput").ap()
    if DEBUG[0] & 16:
        dbg_um = dt("dbg_um", [64, TOK], BF16, kind="ExternalOutput").ap()
    if DEBUG[0] & 32:
        dbg_gt = dt("dbg_gt", [128, 4 * TCH], F32, kind="ExternalOutput").ap()
    if DEBUG[0] & 64:
        dbg_if = dt("dbg_if", [4, TOK], BF16, kind="ExternalOutput").ap()
    if DEBUG[0] & 128:
        dbg_ug = dt("dbg_ug", [68, TOK], F32, kind="ExternalOutput").ap()

    pay_kv = dt("pay_kv", [1, 16 * KVSH], BF16, kind="Internal").ap()
    gout_kv = dt("gout_kv", [8, 16 * KVSH], BF16, kind="Internal").ap()
    pay_q = [dt(f"pay_q{u}", [1, 8 * KVSH], BF16, kind="Internal").ap() for u in range(G)]
    gout_q = [dt(f"gout_q{u}", [8, 8 * KVSH], BF16, kind="Internal").ap() for u in range(G)]
    NPd = max(1, (8 * TOK) // c.QB // 4)
    TPPd = (8 * TOK) // NPd
    pay_o = [dt(f"pay_o{u}", [1, 128 * (8 * TOK)], BF16, kind="Internal").ap()
             for u in range(max(1, G - 1))]
    gout_o = [dt(f"gout_o{u}", [8, 128 * (8 * TOK)], BF16, kind="Internal").ap()
              for u in range(max(1, G - 1))]
    pay_ol = [dt(f"pay_ol{p}", [1, 128 * TPPd], BF16, kind="Internal").ap()
              for p in range(NPd)]
    gout_ol = [dt(f"gout_ol{p}", [8, 128 * TPPd], BF16, kind="Internal").ap()
               for p in range(NPd)]
    gassm = dt("gassm", [8, 128 * (8 * TOK)], BF16, kind="Internal").ap()
    z1d = dt("z1d", [TOK, D], BF16, kind="Internal").ap()

    r = Rec(nc)
    sLDX = r.sem("sLDX")
    sLX = [r.sem("sLX0"), r.sem("sLX1")]
    sLDW = r.sem("sLDW")
    sCON = r.sem("sCON")
    sGPD = r.sem("sGPD")
    sKVX = r.sem("sKVX")
    sVX = r.sem("sVX")
    sQX = [r.sem("sQX0"), r.sem("sQX1")]
    sOX = r.sem("sOX")
    sZ1 = r.sem("sZ1")
    sASM = r.sem("sASM")
    sCC = r.sem("sCC")
    sTE = r.sem("sTE")
    sACT = r.sem("sACT")
    sDVE = r.sem("sDVE")
    sPOOL = r.sem("sPOOL")
    sP = [[r.sem(f"sP{gg}_{pp}") for pp in range(2)] for gg in range(6)]
    sST = r.sem("sST")

    stack = r._stack
    alloc = lambda name, shape, dtp: stack.enter_context(nc.sbuf_tensor(name, shape, dtp))

    BIG8 = alloc("BIG8", [128, KCH * TOK], BF16)
    id32 = alloc("id32", [128, 128], F32)
    idb = alloc("idb", [128, 128], BF16)
    tri = alloc("tri", [128, 128], BF16)
    mk = alloc("mk", [4, 64], BF16)
    onesb = alloc("onesb", [1, 128], BF16)
    onescol = alloc("onescol", [128, 1], BF16)
    onesS = alloc("onesS", [128, 128], BF16)
    bcolv = alloc("bcolv", [128, KCH], F32)
    gbias = alloc("gbias", [68, 1], F32)
    NP = max(1, NB // 4)
    BPP = NB // NP
    TPP = T // NP
    idxK = alloc("idxK", [128, 1], I32)
    idxO = alloc("idxO", [128, 1], I32)
    idxV = alloc("idxV", [128, TCH], I32)
    SQseg = [alloc(f"SQseg{i}", [128, 512], BF16) for i in range(2)]
    SQd = alloc("SQd", [128, D // 2], BF16)
    sumsqb = [alloc(f"sumsqb{i}", [128, 1], F32) for i in range(2)]
    ssum = [alloc(f"ssum{i}", [128, 1], F32) for i in range(2)]
    sumsq = [alloc(f"sumsq{i}", [128, 1], F32) for i in range(2)]
    meanc = [alloc(f"meanc{i}", [128, 1], F32) for i in range(2)]
    m2c = [alloc(f"m2c{i}", [128, 1], F32) for i in range(2)]
    varc = [alloc(f"varc{i}", [128, 1], F32) for i in range(2)]
    lnvc = [alloc(f"lnvc{i}", [128, 1], F32) for i in range(2)]
    rstdc = [alloc(f"rstdc{i}", [128, 1], F32) for i in range(2)]
    negmrc = [alloc(f"negmrc{i}", [128, 1], F32) for i in range(2)]
    segrow = alloc("segrow", [1, 512], BF16)
    segmax = alloc("segmax", [1, 1], F32)
    kn2m = alloc("kn2m", [1, 1], F32)
    qn2m = alloc("qn2m", [1, 1], F32)
    mtmp = alloc("mtmp", [1, 1], F32)
    mtmp2 = alloc("mtmp2", [1, 1], F32)
    negmb = alloc("negmb", [1, 1], BF16)
    negmcol = alloc("negmcol", [128, G], F32)

    PSA = stack.enter_context(nc.psum_tensor("PSA", [128, 1024], F32))
    PSB = stack.enter_context(nc.psum_tensor("PSB", [128, 1024], F32))
    PSC = stack.enter_context(nc.psum_tensor("PSC", [128, 1024], F32))
    PSE = stack.enter_context(nc.psum_tensor("PSE", [128, 512], F32))
    PSDh = [None]

    p12 = contextlib.ExitStack()
    a12 = lambda name, shape, dtp: p12.enter_context(nc.sbuf_tensor(name, shape, dtp))
    KTc = a12("KTc", [128, T], BF16)
    Vc = a12("Vc", [128, T], BF16)
    Qb0 = a12("Qb0", [128, T], BF16)
    p1 = contextlib.ExitStack()
    a1 = lambda name, shape, dtp: p1.enter_context(nc.sbuf_tensor(name, shape, dtp))
    PSD1 = p1.enter_context(nc.psum_tensor("PSD1", [128, 1024], BF16))
    PSDh[0] = PSD1
    cosT = a1("cosT", [128, TOK], F32)
    sinT = a1("sinT", [128, TOK], F32)
    xt = [a1(f"xt{i}", [128, D], F32) for i in range(2)]
    xnb0 = a1("xnb0", [128, D], BF16)
    xnb = [xnb0, xnb0]
    wnat = a1("wnat", [128, D], F32)
    lhsT = [a1(f"lhsT{i}", [128, KCH * 128], BF16) for i in range(2)]
    STG = [a1(f"STG{i}", [128, TOK], BF16) for i in range(2)]
    RT1 = a1("RT1", [128, TOK], BF16)
    VTM = RT1

    for dst, src in [(id32, id32d), (idb, idbd), (tri, trid), (mk, mkd),
                     (cosT, cosd), (sinT, sind), (idxK, idxKd), (idxO, idxOd),
                     (idxV, idxVd)]:
        r.op("sync", lambda e, dst=dst, src=src: e.dma_start(out=dst[:, :], in_=src[:, :]),
             inc=sCON, inc_amt=16)
    r.op("sync", lambda e: e.dma_start(out=bcolv[:, :],
                                       in_=bf_in.rearrange("(m p) -> p m", p=128),
                                       allow_slow_non_contiguous=True),
         inc=sCON, inc_amt=16)
    r.op("vector", lambda e: e.memset(gbias[:, :], 0.0), inc=sDVE)
    r.op("sync", lambda e: e.dma_start(out=gbias[64:68, 0:1],
                                       in_=gb.rearrange("(g o) -> g o", o=1)),
         wait=[(sDVE, sDVE.n)], inc=sCON, inc_amt=16)
    r.op("vector", lambda e: e.memset(onesb[:, :], 1.0), inc=sDVE)
    r.op("vector", lambda e: e.memset(onescol[:, :], 1.0), inc=sDVE)
    r.op("vector", lambda e: e.memset(onesS[:, :], SUMSCALE), inc=sDVE)
    CONSTS_LDW = sCON.n
    CONSTS_DVE = sDVE.n

    ln_act_tok = {0: 0, 1: 0}
    ln_dve_tok = {0: 0, 1: 0}
    ln_pool_tok = {0: 0, 1: 0}

    def ln_chain(src_big, apply_out, par=0, extra_act=(), extra_dve=(), extra_apply=()):
        p = par % 2
        su, sq, mn, m2, va, lv, rs, ng = (ssum[p], sumsq[p], meanc[p], m2c[p],
                                          varc[p], lnvc[p], rstdc[p], negmrc[p])
        sqb = sumsqb[p]
        r.op("vector", lambda e: e.reduce_sum(su[:, :], src_big, AXX),
             wait=[(sDVE, ln_dve_tok[p])] + list(extra_dve), inc=sDVE)
        r.op("scalar", lambda e: e.activation(SQd[:, :], src_big[:, 0:D // 2], AF.Square,
                                              bias=0.0, scale=1.0, accum_out=sq[:, 0:1]),
             wait=[(sACT, ln_act_tok[p]), (sDVE, ln_dve_tok[p])] + list(extra_act), inc=sACT)
        r.op("scalar", lambda e: e.activation(SQd[:, :], src_big[:, D // 2:D], AF.Square,
                                              bias=0.0, scale=1.0, accum_out=sqb[:, 0:1]),
             wait=[(sACT, sACT.n)], inc=sACT)
        SQTOK = sACT.n
        r.op("vector", lambda e: e.tensor_add(sq[:, :], sq[:, :], sqb[:, :]),
             wait=[(sACT, SQTOK), (sDVE, sDVE.n)], inc=sDVE)
        SQDVE = sDVE.n
        r.op("vector", lambda e: e.tensor_scalar_mul(mn[:, :], su[:, :], 1.0 / D),
             wait=[(sDVE, sDVE.n)], inc=sDVE)
        r.op("vector", lambda e: e.tensor_mul(m2[:, :], mn[:, :], mn[:, :]),
             wait=[(sDVE, sDVE.n)], inc=sDVE)
        r.op("vector", lambda e: e.tensor_scalar_add(m2[:, :], m2[:, :], -c.EPS),
             wait=[(sDVE, sDVE.n)], inc=sDVE)
        r.op("vector", lambda e: e.scalar_tensor_tensor(va[:, :], sq[:, :], 1.0 / D,
                                                        m2[:, :], ALU.mult, ALU.subtract),
             wait=[(sDVE, sDVE.n)], inc=sDVE)
        r.op("scalar", lambda e: e.activation(lv[:, :], va[:, :], AF.Ln, bias=0.0,
                                              scale=1.0),
             wait=[(sDVE, sDVE.n)], inc=sACT)
        r.op("scalar", lambda e: e.activation(rs[:, :], lv[:, :], AF.Exp, bias=0.0,
                                              scale=-0.5),
             wait=[(sACT, sACT.n), (sPOOL, ln_pool_tok[p])], inc=sACT)
        r.op("vector", lambda e: e.scalar_tensor_tensor(ng[:, :], mn[:, :], -1.0,
                                                        rs[:, :], ALU.mult, ALU.mult),
             wait=[(sACT, sACT.n), (sPOOL, ln_pool_tok[p])], inc=sDVE)
        r.op("gpsimd", lambda e: e.tensor_scalar(apply_out, src_big, rs[:, 0:1],
                                                 ng[:, 0:1], ALU.mult, ALU.add),
             wait=[(sDVE, sDVE.n), (sPOOL, ln_pool_tok[p])] + list(extra_apply), inc=sPOOL)
        ln_act_tok[p] = sACT.n
        ln_dve_tok[p] = sDVE.n
        ln_pool_tok[p] = sPOOL.n

    first_te = [True]

    def tm_to_fm(src_tile, dest_view_fn, i, wait_src):
        PSD = PSDh[0]
        for jg in range(KCH // 8):
            for jj in range(8):
                j = jg * 8 + jj
                w = []
                if jj == 0:
                    w = list(wait_src) + [(sDVE, sDVE.n)]
                    if first_te[0]:
                        w += [(sCON, CONSTS_LDW), (sDVE, CONSTS_DVE)]
                        first_te[0] = False
                r.op("tensor", lambda e, j=j, jj=jj, PSD=PSD: e.transpose(
                    PSD[:, jj * 128:(jj + 1) * 128],
                    src_tile[:, j * 128:(j + 1) * 128], idb[:]),
                    wait=w, inc=(sTE if jj == 7 else None))
            r.op("vector", lambda e, jg=jg, i=i, PSD=PSD: e.tensor_copy(
                dest_view_fn(jg, i),
                PSD[:, :].rearrange("p (j t) -> p j t", t=128)),
                wait=[(sTE, sTE.n)], inc=sDVE)

    def big8_dest(jg, i):
        v = BIG8[:, :].rearrange("p (j t) -> p j t", t=TOK)
        return v[:, jg * 8:(jg + 1) * 8, i * 128:i * 128 + 128]

    def run_pipelined(items, w_nat_pair, lhsT_pair):
        nw = len(w_nat_pair)
        tr_tok = {}
        evac_tok = {}

        def prep_dma(i):
            it = items[i]
            w = [(sTE, tr_tok[i - nw])] if i >= nw else [(sTE, sTE.n)]
            wn = w_nat_pair[i % nw]
            r.op("sync", lambda e, it=it, wn=wn: e.dma_start(
                out=wn[0:it["nrows"], :],
                in_=it["weight"][it["row"] * 128: it["row"] * 128 + it["nrows"], :]),
                wait=w, inc=sLDW, inc_amt=16)
            it["ldw"] = sLDW.n

        psc_tok = {}

        def prep_tr(i):
            it = items[i]
            nrows = it["nrows"]
            dst = lhsT_pair[i % 2]
            wn = w_nat_pair[i % nw]
            ngrp = (KCH + 3) // 4
            for kg in range(ngrp):
                nk = min(4, KCH - kg * 4)
                for kk in range(nk):
                    k = kg * 4 + kk
                    w = [(sLDW, it["ldw"])] if k == 0 else []
                    if kk == 0:
                        prev = psc_tok.get((kg - 2) % max(ngrp, 2) if kg >= 2 else ("last", kg % 2))
                        if prev:
                            w.append((sDVE, prev))
                    half = kg % 2
                    r.op("tensor", lambda e, k=k, kk=kk, half=half, nrows=nrows, wn=wn:
                         e.transpose(PSC[0:128, half * 512 + kk * 128: half * 512 + (kk + 1) * 128],
                                     wn[0:nrows, k * 128:(k + 1) * 128], id32[:]),
                         wait=w, inc=(sTE if kk == nk - 1 else None))
                r.op("vector", lambda e, kg=kg, nk=nk, dst=dst: e.tensor_copy(
                    dst[0:128, kg * 512: kg * 512 + nk * 128],
                    PSC[:, (kg % 2) * 512: (kg % 2) * 512 + nk * 128]),
                    wait=[(sTE, sTE.n)], inc=sDVE)
                psc_tok[kg] = sDVE.n
            psc_tok[("last", (ngrp - 1) % 2)] = psc_tok[ngrp - 1]
            if ngrp >= 2:
                psc_tok[("last", (ngrp - 2) % 2)] = psc_tok[ngrp - 2]
            tr_tok[i] = sTE.n
            evac_tok[i] = sDVE.n

        prep_dma(0)
        prep_tr(0)
        for i, it in enumerate(items):
            if i + 1 < len(items):
                prep_dma(i + 1)
            it["body"](lhsT_pair[i % 2], evac_tok[i])
            it["post"](i % 2)
            if i + 1 < len(items):
                prep_tr(i + 1)
            if "extra" in it:
                it["extra"]()

    rope_first = [True]

    def rope(stage):
        w = [(sACT, sACT.n)] + [(sm, sm.n) for gg in range(6) for sm in sP[gg] if sm.n > 0]
        if rope_first[0]:
            w.append((sCON, CONSTS_LDW))
            rope_first[0] = False
        r.op("vector", lambda e, stage=stage: e.tensor_mul(
            RT1[0:64, :], stage[64:128, :], sinT[64:128, :]), wait=w, inc=sDVE)
        r.op("vector", lambda e, stage=stage: e.tensor_mul(
            RT1[64:128, :], stage[0:64, :], sinT[0:64, :]), inc=sDVE)
        r.op("vector", lambda e, stage=stage: e.tensor_mul(
            stage[:, :], stage[:, :], cosT[:, :]), wait=[(sDVE, sDVE.n)], inc=sDVE)
        r.op("vector", lambda e, stage=stage: e.tensor_add(
            stage[:, :], stage[:, :], RT1[:, :]), wait=[(sDVE, sDVE.n)], inc=sDVE)

    def pay_dma(fn, grp, par):
        sem = sP[grp][par]
        r.op("sync", fn, wait=[(sDVE, sDVE.n)], inc=sem, inc_amt=16)
        return (sem, sem.n)

    for i in range(TCH):
        buf = i % 2
        w = [(sDVE, sDVE.n), (sACT, sACT.n), (sPOOL, sPOOL.n)] if i >= 2 else []
        r.op("sync", lambda e, i=i, buf=buf: e.dma_start(
            out=xt[buf][:, :], in_=x_in[i * 128:(i + 1) * 128, :]),
            wait=w, inc=sLX[buf], inc_amt=16)
        ln_chain(xt[buf][:, :], xnb[buf][:, :], par=i,
                 extra_act=[(sLX[buf], sLX[buf].n)], extra_dve=[(sLX[buf], sLX[buf].n)],
                 extra_apply=[(sTE, sTE.n)])
        tm_to_fm(xnb[buf], big8_dest, i, wait_src=[(sPOOL, sPOOL.n)])

    psum_tok = {0: 0, 1: 0}
    stg_tok = {0: (sLDX, 0), 1: (sLDX, 0)}

    def proj_body(psum, ibuf):
        def body(lhsT_buf, evtok):
            for tb in range(NTB):
                for k in range(KCH):
                    w = []
                    if tb == 0 and k == 0:
                        w = [(sDVE, evtok), (sACT, psum_tok[ibuf])]
                    r.op("tensor", lambda e, tb=tb, k=k, lhsT_buf=lhsT_buf, psum=psum: e.matmul(
                        psum[:, tb * PTB:(tb + 1) * PTB],
                        lhsT_buf[:, k * 128:(k + 1) * 128],
                        BIG8[:, k * TOK + tb * PTB: k * TOK + (tb + 1) * PTB],
                        start=(k == 0), stop=(k == KCH - 1)),
                        wait=w, inc=(sTE if (k == KCH - 1 and tb == NTB - 1) else None))
        return body

    CC_Q = []
    CC_KV_box = [0]

    def k_post(g, ibuf):
        def post(buf):
            psum = PSA if ibuf == 0 else PSB
            r.op("scalar", lambda e: e.activation(STG[buf][:, :], psum[:, 0:TOK], AF.Copy,
                                                  bias=0.0, scale=1.0),
                 wait=[(sTE, sTE.n), stg_tok[ibuf]], inc=sACT)
            psum_tok[ibuf] = r.sems["sACT"].n
            rope(STG[buf])
            stg_tok[ibuf] = pay_dma(lambda e, g=g, buf=buf: e.dma_start(
                out=pay_kv[0:1, g * KVSH:(g + 1) * KVSH].rearrange("o (p t) -> p (o t)", p=128),
                in_=STG[buf][:, :]), 0, ibuf)
        return post

    def v_post(g, ibuf):
        def post(buf):
            psum = PSA if ibuf == 0 else PSB
            PSD = PSDh[0]
            r.op("scalar", lambda e: e.activation(STG[buf][:, :], psum[:, 0:TOK], AF.Copy,
                                                  bias=0.0, scale=1.0),
                 wait=[(sTE, sTE.n), stg_tok[ibuf]], inc=sACT)
            psum_tok[ibuf] = r.sems["sACT"].n
            for j in range(TCH):
                w = [(sACT, sACT.n), (sDVE, sDVE.n)] if j == 0 else []
                r.op("tensor", lambda e, j=j, buf=buf, PSD=PSD: e.transpose(
                    PSD[:, j * 128:(j + 1) * 128], STG[buf][:, j * 128:(j + 1) * 128], idb[:]),
                    wait=w, inc=(sTE if j == TCH - 1 else None))
            r.op("vector", lambda e, PSD=PSD: e.tensor_copy(VTM[:, :], PSD[:, 0:TOK]),
                 wait=[(sTE, sTE.n)] + [(sm, sm.n) for gg in range(6) for sm in sP[gg] if sm.n > 0],
                 inc=sDVE)
            stg_tok[ibuf] = pay_dma(lambda e, g=g: e.dma_start(
                out=pay_kv[0:1, 8 * KVSH + g * KVSH: 8 * KVSH + (g + 1) * KVSH]
                    .rearrange("o (j p cc) -> p o j cc", p=128, cc=128),
                in_=VTM[:, :].rearrange("p (j cc) -> p j cc", cc=128)), 0, ibuf)
            if g == 7:
                r.op("gpsimd", lambda e: e.collective_compute(
                    "AllGather", ALU.bypass, replica_groups=[list(range(8))],
                    ins=[pay_kv.opt()], outs=[gout_kv.opt()]),
                    wait=[(sP[0][0], sP[0][0].n), (sP[0][1], sP[0][1].n)], inc=sCC)
                CC_KV_box[0] = sCC.n
        return post

    def q_post(u, g, ibuf):
        def post(buf):
            psum = PSA if ibuf == 0 else PSB
            r.op("scalar", lambda e: e.activation(STG[buf][:, :], psum[:, 0:TOK], AF.Copy,
                                                  bias=0.0, scale=1.0),
                 wait=[(sTE, sTE.n), stg_tok[ibuf]], inc=sACT)
            psum_tok[ibuf] = r.sems["sACT"].n
            rope(STG[buf])
            stg_tok[ibuf] = pay_dma(lambda e, u=u, g=g, buf=buf: e.dma_start(
                out=pay_q[u][0:1, g * KVSH:(g + 1) * KVSH].rearrange("o (p t) -> p (o t)", p=128),
                in_=STG[buf][:, :]), 1 + u, buf)
            if g == 7:
                r.op("gpsimd", lambda e, u=u: e.collective_compute(
                    "AllGather", ALU.bypass, replica_groups=[list(range(8))],
                    ins=[pay_q[u].opt()], outs=[gout_q[u].opt()]),
                    wait=[(sP[1 + u][0], sP[1 + u][0].n), (sP[1 + u][1], sP[1 + u][1].n)], inc=sCC)
                CC_Q.append(sCC.n)
        return post

    items = []
    for g in range(8):
        i = len(items)
        items.append(dict(weight=Wk, row=g, nrows=128,
                          body=proj_body(PSA if i % 2 == 0 else PSB, i % 2), post=k_post(g, i % 2)))
        i = len(items)
        items.append(dict(weight=Wv, row=g, nrows=128,
                          body=proj_body(PSA if i % 2 == 0 else PSB, i % 2), post=v_post(g, i % 2)))
    for u in range(G):
        for g in range(8):
            i = len(items)
            items.append(dict(weight=Wq, row=g * G + u, nrows=128,
                              body=proj_body(PSA if i % 2 == 0 else PSB, i % 2),
                              post=q_post(u, g, i % 2)))
    import collections
    kv_done_box = [0]
    q0_done_box = [0]
    negm_tok = {}

    def k_extract_task(rr):
        def t():
            r.op("gpsimd", lambda e, rr=rr: e.indirect_dma_start(
                out=KTc[:, rr * TOK:(rr + 1) * TOK], out_offset=None,
                in_=gout_kv.rearrange("r (rw t) -> (r rw) t", t=TOK),
                in_offset=IOA(idxK[:, 0:1], 0),
                element_offset=rr * 16 * KVSH),
                wait=[(sCC, CC_KV_box[0])], inc=sKVX, inc_amt=16)
        return t

    def v_extract_task(rr, j):
        def t():
            r.op("gpsimd", lambda e, rr=rr, j=j: e.indirect_dma_start(
                out=Vc[:, (rr * TCH + j) * 128:(rr * TCH + j + 1) * 128], out_offset=None,
                in_=gout_kv.rearrange("r (rw cc) -> (r rw) cc", cc=128),
                in_offset=IOA(idxV[:, j:j + 1], 0),
                element_offset=rr * 16 * KVSH + 8 * KVSH),
                wait=[(sCC, CC_KV_box[0])], inc=sVX, inc_amt=16)
            kv_done_box[0] = r.sems["sVX"].n
        return t

    def q_extract_task(u, rr, qbuf):
        def t():
            r.op("gpsimd", lambda e, rr=rr, u=u, qbuf=qbuf: e.indirect_dma_start(
                out=qbuf[:, rr * TOK:(rr + 1) * TOK], out_offset=None,
                in_=gout_q[u].rearrange("r (rw t) -> (r rw) t", t=TOK),
                in_offset=IOA(idxK[:, 0:1], 0),
                element_offset=rr * 8 * KVSH),
                wait=[(sCC, CC_Q[u])], inc=sQX[u % 2], inc_amt=16)
            if rr == 7:
                q0_done_box[0] = r.sems["sQX0"].n
        return t

    def norm_steps(src_full, dst_scalar, pre_wait_fn):
        nseg = T // 512
        sq_tok = {}
        mm_tok = {}
        steps = []
        for sg in range(nseg):
            def ta(sg=sg):
                sseg = SQseg[sg % 2]
                w = list(pre_wait_fn()) if sg == 0 else []
                if sg >= 2:
                    w.append((sTE, mm_tok[sg - 2]))
                r.op("scalar", lambda e, sg=sg, sseg=sseg: e.activation(
                    sseg[:, :], src_full[:, sg * 512:(sg + 1) * 512], AF.Square,
                    bias=0.0, scale=1.0),
                    wait=w, inc=sACT)
                sq_tok[sg] = sACT.n

            def tb(sg=sg):
                sseg = SQseg[sg % 2]
                wm = [(sACT, sq_tok[sg])]
                if sg == 0:
                    wm.append((sDVE, sDVE.n))
                r.op("tensor", lambda e, sseg=sseg: e.matmul(
                    PSE[0:1, 0:512], onescol[:, 0:1], sseg[:, :],
                    start=True, stop=True),
                    wait=wm, inc=sTE)
                mm_tok[sg] = sTE.n
                r.op("scalar", lambda e: e.copy(segrow[:, :], PSE[0:1, 0:512]),
                     wait=[(sTE, mm_tok[sg]), (sDVE, sDVE.n)], inc=sACT)
                r.op("vector", lambda e: e.reduce_max(segmax[:, :], segrow[:, :], AXX),
                     wait=[(sACT, sACT.n), (sDVE, sDVE.n)], inc=sDVE)
                if sg == 0:
                    r.op("vector", lambda e: e.tensor_copy(dst_scalar, segmax[:, :]),
                         wait=[(sDVE, sDVE.n)], inc=sDVE)
                else:
                    r.op("vector", lambda e, dst_scalar=dst_scalar: e.tensor_max(
                        dst_scalar, dst_scalar, segmax[:, :]),
                        wait=[(sDVE, sDVE.n)], inc=sDVE)
            steps.append(ta)
            steps.append(tb)
        return steps

    def negm_chain(u):
        r.op("vector", lambda e: e.tensor_mul(mtmp[:, :], qn2m[:, :], kn2m[:, :]),
             wait=[(sDVE, sDVE.n)], inc=sDVE)
        r.op("scalar", lambda e: e.activation(mtmp2[:, :], mtmp[:, :], AF.Ln, bias=0.0,
                                              scale=1.0),
             wait=[(sDVE, sDVE.n)], inc=sACT)
        r.op("scalar", lambda e: e.activation(mtmp[:, :], mtmp2[:, :], AF.Exp, bias=0.0,
                                              scale=0.5),
             wait=[(sACT, sACT.n)], inc=sACT)
        r.op("vector", lambda e: e.tensor_scalar_mul(mtmp2[:, :], mtmp[:, :], -c.SCALE),
             wait=[(sACT, sACT.n)], inc=sDVE)
        r.op("vector", lambda e: e.tensor_copy(negmb[:, :], mtmp2[:, :]),
             wait=[(sDVE, sDVE.n)], inc=sDVE)
        r.op("tensor", lambda e: e.matmul(PSE[0:128, 0:1], onesb[0:1, :], negmb[0:1, 0:1],
                                          start=True, stop=True),
             wait=[(sDVE, sDVE.n), (sACT, sACT.n)], inc=sTE)
        r.op("scalar", lambda e, u=u: e.copy(negmcol[:, u:u + 1], PSE[0:128, 0:1]),
             wait=[(sTE, sTE.n)], inc=sACT)
        negm_tok[u] = sACT.n

    qA = collections.deque()
    for rr in range(8):
        qA.append(k_extract_task(rr))
    qA.extend(norm_steps(KTc[:, :], kn2m[:, :], lambda: [(sKVX, sKVX.n)]))
    for rr in range(8):
        for j in range(TCH):
            qA.append(v_extract_task(rr, j))
    qB = collections.deque()
    for rr in range(8):
        qB.append(q_extract_task(0, rr, Qb0))
    qB.extend(norm_steps(Qb0[:, :], qn2m[:, :], lambda: [(sQX[0], q0_done_box[0])]))
    qB.append(lambda: negm_chain(0))

    nitems = len(items)
    for i, it in enumerate(items):
        def extra(i=i, it=it):
            if i >= 16:
                budget = 4
                while qA and budget > 0:
                    qA.popleft()()
                    budget -= 1
            if i >= 24 and not qA:
                budget = 2
                while qB and budget > 0:
                    qB.popleft()()
                    budget -= 1
        it["extra"] = extra
    run_pipelined(items, [wnat], lhsT)
    CC_KV = CC_KV_box[0]
    while qA:
        qA.popleft()()
    while qB:
        qB.popleft()()
    KVEX = kv_done_box[0]

    r.barrier()
    p1.close()

    p2 = contextlib.ExitStack()
    a2 = lambda name, shape, dtp: p2.enter_context(nc.sbuf_tensor(name, shape, dtp))
    PSF = p2.enter_context(nc.psum_tensor("PSF", [128, 512], F32))
    Qb1 = a2("Qb1", [128, T], BF16)
    Qb = [Qb0, Qb1]
    OTb = [a2(f"OTb{i}", [128, T], BF16) for i in range(2)]
    PT = [a2(f"PT{i}", [128, 1024], BF16) for i in range(2)]
    recipb = a2("recipb", [128, 512], F32)

    prev_last_exp = [0]
    stt_tok = {0: 0, 1: 0}
    recip_tok = {0: 0, 1: 0}
    cc_o_tok = {}
    asm_tok = {}
    qex_tok = {0: q0_done_box[0]}
    scores_done = {-1: 0, 0: 0, 1: 0, 2: 0, 3: 0}

    def q_extract_mid(u, rr):
        ubuf2 = u % 2
        w = [(sCC, CC_Q[u])]
        if rr == 0 and u >= 2:
            w.append((sTE, scores_done[u - 2]))
        r.op("gpsimd", lambda e, rr=rr, u=u, ubuf2=ubuf2: e.indirect_dma_start(
            out=Qb[ubuf2][:, rr * TOK:(rr + 1) * TOK], out_offset=None,
            in_=gout_q[u].rearrange("r (rw t) -> (r rw) t", t=TOK),
            in_offset=IOA(idxK[:, 0:1], 0),
            element_offset=rr * 8 * KVSH),
            wait=w, inc=sQX[u % 2], inc_amt=16)

    oex_done = {}

    def o_extract(u):
        src = gassm if u == G - 1 else gout_o[u]
        wv = [(sCC, cc_o_tok[u])] if u < G - 1 else [(sASM, asm_tok[NP - 1])]
        for rr in range(8):
            r.op("gpsimd", lambda e, u=u, rr=rr, src=src: e.indirect_dma_start(
                out=BIG8[:, (rr * G + u) * TOK:(rr * G + u + 1) * TOK], out_offset=None,
                in_=src.rearrange("r (rw t) -> (r rw) t", t=TOK),
                in_offset=IOA(idxO[:, 0:1], 0),
                element_offset=rr * 128 * T),
                wait=(wv if rr == 0 else []), inc=sOX, inc_amt=16)
        oex_done[u] = sOX.n

    for u in range(G):
        ubuf = u % 2
        if u >= 1:
            for st in norm_steps(Qb[ubuf][:, :], qn2m[:, :],
                                 lambda: [(sQX[u % 2], qex_tok[u]), (sDVE, sDVE.n)]):
                st()
            negm_chain(u)
        QEX = qex_tok[u]
        NEGM_ACT = negm_tok[u]

        for bq in range(NB):
            lo = min(4, NB - 1)
            if u + 1 < G and lo <= bq < lo + 8 and NB >= lo + 8:
                q_extract_mid(u + 1, bq - lo)
                if bq == lo + 7:
                    qex_tok[u + 1] = sQX[(u + 1) % 2].n
            elif u + 1 < G and NB < lo + 8 and bq == 0:
                for rr2 in range(8):
                    q_extract_mid(u + 1, rr2)
                qex_tok[u + 1] = sQX[(u + 1) % 2].n
            nvis = DPB * bq + DPB
            npair = nvis // 2
            obank = bq % 2
            sbank = PSE if bq % 2 == 0 else PSF
            EXP_ACT = {}
            AVT = {}
            MSKS = {}

            def emit_avsums(pr, bq=bq, nvis=nvis, obank=obank, sbank=sbank,
                            EXP_ACT=EXP_ACT, AVT=AVT, MSKS=MSKS, QEX=QEX,
                            NEGM_ACT=NEGM_ACT):
                ptb = PT[pr % 2]
                for half in range(2):
                    sc = pr * 2 + half
                    w = [(sACT, EXP_ACT[pr])]
                    if sc in MSKS:
                        w.append((sPOOL, MSKS[sc]))
                    if sc == 0:
                        w.append((sDVE, stt_tok[obank]))
                        if bq == 0:
                            w.append((sVX, KVEX))
                    r.op("tensor", lambda e, sc=sc, half=half, ptb=ptb: e.matmul(
                        PSC[:, obank * 512: obank * 512 + 512],
                        Vc[:, sc * 128:(sc + 1) * 128],
                        ptb[:, half * 512:(half + 1) * 512],
                        start=(sc == 0), stop=(sc == nvis - 1)),
                        wait=w)
                    w2 = []
                    if sc == 0:
                        w2 = [(sDVE, recip_tok[bq % 2])]
                        if bq == 0:
                            w2.append((sACT, NEGM_ACT))
                    r.op("tensor", lambda e, sc=sc, half=half, ptb=ptb: e.matmul(
                        sbank[:, 0:512], onesS[:, :],
                        ptb[:, half * 512:(half + 1) * 512],
                        start=(sc == 0), stop=(sc == nvis - 1)),
                        wait=w2, inc=(sTE if half == 1 else None))
                AVT[pr] = sTE.n

            pending = None
            for pr in range(npair):
                ps = PSA if pr % 2 == 0 else PSB
                for half in range(2):
                    sc = pr * 2 + half
                    w = []
                    if half == 0:
                        if pr >= 2:
                            w.append((sACT, EXP_ACT[pr - 2]))
                        else:
                            w.append((sACT, prev_last_exp[0]))
                        if pr == 0 and bq == 0:
                            w += [(sQX[u % 2], QEX), (sACT, NEGM_ACT)]
                    r.op("tensor", lambda e, sc=sc, half=half, ps=ps, bq=bq, ubuf=ubuf: e.matmul(
                        ps[:, half * 512:(half + 1) * 512],
                        KTc[:, sc * 128:(sc + 1) * 128],
                        Qb[ubuf][:, bq * QB: bq * QB + 512],
                        start=True, stop=True),
                        wait=w, inc=(sTE if half == 1 else None))
                SC_TE = sTE.n
                ptb = PT[pr % 2]
                w = [(sTE, SC_TE)]
                if pr >= 2:
                    w.append((sTE, AVT[pr - 2]))
                if pr == 0 and bq == 0:
                    w.append((sACT, NEGM_ACT))
                r.op("scalar", lambda e, ps=ps, ptb=ptb, u=u: e.activation(
                    ptb[:, :], ps[:, :], AF.Exp, bias=negmcol[:, u:u + 1], scale=c.SCALE),
                    wait=w, inc=sACT)
                EXP_ACT[pr] = sACT.n
                for half in range(2):
                    sc = pr * 2 + half
                    m = sc - DPB * bq
                    if m >= 0:
                        if m > 0:
                            r.op("gpsimd", lambda e, ptb=ptb, half=half, m=m: e.memset(
                                ptb[:, half * 512: half * 512 + m * 128], 0.0),
                                wait=[(sACT, EXP_ACT[pr])])
                        r.op("gpsimd", lambda e, ptb=ptb, half=half, m=m: e.tensor_mul(
                            ptb[:, half * 512 + m * 128: half * 512 + (m + 1) * 128],
                            ptb[:, half * 512 + m * 128: half * 512 + (m + 1) * 128],
                            tri[:, :]),
                            wait=[(sACT, EXP_ACT[pr])], inc=sPOOL)
                        MSKS[sc] = sPOOL.n
                if pending is not None:
                    emit_avsums(pending)
                pending = pr
            emit_avsums(pending)
            prev_last_exp[0] = EXP_ACT[npair - 1]
            r.op("vector", lambda e, sbank=sbank: e.reciprocal(recipb[:, :], sbank[:, 0:512]),
                 wait=[(sTE, AVT[npair - 1]), (sDVE, sDVE.n)], inc=sDVE)
            recip_tok[bq % 2] = sDVE.n
            wstt = [(sDVE, sDVE.n)]
            if bq == 0:
                wstt.append((sP[5][ubuf], sP[5][ubuf].n))
            r.op("vector", lambda e, ubuf=ubuf, bq=bq, obank=obank: e.scalar_tensor_tensor(
                OTb[ubuf][:, bq * QB: bq * QB + 512], PSC[:, obank * 512: obank * 512 + 512],
                SUMSCALE, recipb[:, :], ALU.mult, ALU.mult),
                wait=wstt, inc=sDVE)
            stt_tok[obank] = sDVE.n
            if u == G - 1 and (bq + 1) % BPP == 0:
                p = bq // BPP
                psem = sP[5][p % 2]
                r.op("sync", lambda e, ubuf=ubuf, p=p, psem=psem: e.dma_start(
                    out=pay_ol[p][0:1, :].rearrange("o (pp t) -> pp (o t)", pp=128),
                    in_=OTb[ubuf][:, p * TPP:(p + 1) * TPP]),
                    wait=[(sDVE, sDVE.n)], inc=psem, inc_amt=16)
                r.op("gpsimd", lambda e, p=p, psem=psem: e.collective_compute(
                    "AllGather", ALU.bypass, replica_groups=[list(range(8))],
                    ins=[pay_ol[p].opt()], outs=[gout_ol[p].opt()]),
                    wait=[(psem, psem.n)], inc=sCC)
                cctok = r.sems["sCC"].n
                r.op("sync", lambda e, p=p: e.dma_start(
                    out=gassm.rearrange("r (rw pp t) -> r rw pp t", pp=NP, t=TPP)[:, :, p, :],
                    in_=gout_ol[p].rearrange("r (rw t) -> r rw t", t=TPP)),
                    wait=[(sCC, cctok)], inc=sASM, inc_amt=16)
                asm_tok[p] = r.sems["sASM"].n
        if u < G - 1:
            psem = sP[5][u % 2]
            r.op("sync", lambda e, u=u, ubuf=ubuf: e.dma_start(
                out=pay_o[u][0:1, :].rearrange("o (pp t) -> pp (o t)", pp=128),
                in_=OTb[ubuf][:, :]),
                wait=[(sDVE, sDVE.n)], inc=psem, inc_amt=16)
            r.op("gpsimd", lambda e, u=u, psem=psem: e.collective_compute(
                "AllGather", ALU.bypass, replica_groups=[list(range(8))],
                ins=[pay_o[u].opt()], outs=[gout_o[u].opt()]),
                wait=[(psem, psem.n)], inc=sCC)
            cc_o_tok[u] = sCC.n
        scores_done[u] = r.sems["sTE"].n
        if u >= 1:
            o_extract(u - 1)
    o_extract(G - 1)
    OEX = sOX.n

    r.barrier()
    p2.close()
    p12.close()

    p3o = contextlib.ExitStack()
    BIGB = p3o.enter_context(nc.sbuf_tensor("BIGB", [128, TCH * D], BF16))
    PSD3 = p3o.enter_context(nc.psum_tensor("PSD3", [128, 1024], BF16))
    PSDh[0] = PSD3
    p3 = contextlib.ExitStack()
    a3 = lambda name, shape, dtp: p3.enter_context(nc.sbuf_tensor(name, shape, dtp))
    wnat3 = [a3("wnat3", [128, D], F32)]
    lhsT3 = [a3(f"lhsT3{i}", [128, KCH * 128], BF16) for i in range(2)]
    YST = [a3(f"YST{i}", [128, TOK], BF16) for i in range(2)]
    xt3 = a3("xt3", [128, D], F32)
    xnb3 = a3("xnb3", [128, D], BF16)

    def dest_tm(dest_big, m):
        v = dest_big[:, :].rearrange("p (j d) -> p j d", d=D)
        return v[:, :, m * 128:(m + 1) * 128]

    fm_psum_tok = {0: 0, 1: 0}

    def fm_items(weight_ap, stg_pair, dest_big, bias_col=None, extra_mm=None,
                 first_wait=(), first_evac_wait=()):
        its = []
        for m in range(KCH):
            def body(lhsT_buf, evtok, m=m):
                psum = PSA if m % 2 == 0 else PSB
                for tb in range(NTB):
                    for k in range(KCH):
                        w = []
                        if tb == 0 and k == 0:
                            w = [(sDVE, evtok), (sACT, fm_psum_tok[m % 2])]
                            if m == 0:
                                w += list(first_wait)
                        last = (k == KCH - 1) and (extra_mm is None)
                        r.op("tensor", lambda e, tb=tb, k=k, lhsT_buf=lhsT_buf, last=last, psum=psum: e.matmul(
                            psum[:, tb * PTB:(tb + 1) * PTB],
                            lhsT_buf[:, k * 128:(k + 1) * 128],
                            BIG8[:, k * TOK + tb * PTB: k * TOK + (tb + 1) * PTB],
                            start=(k == 0), stop=last),
                            wait=w,
                            inc=(sTE if (last and tb == NTB - 1) else None))
                    if extra_mm is not None:
                        extra_mm(m, tb, psum)

            def post(buf, m=m):
                psum = PSA if m % 2 == 0 else PSB
                PSD = PSDh[0]
                sbuf = m % 2
                if bias_col is None:
                    r.op("scalar", lambda e, psum=psum, sbuf=sbuf: e.activation(
                        stg_pair[sbuf][:, :], psum[:, 0:TOK], AF.Copy, bias=0.0, scale=1.0),
                        wait=[(sTE, sTE.n)], inc=sACT)
                else:
                    r.op("scalar", lambda e, psum=psum, sbuf=sbuf, m=m: e.activation(
                        stg_pair[sbuf][:, :], psum[:, 0:TOK], AF.Identity,
                        bias=bias_col[:, m:m + 1], scale=1.0),
                        wait=[(sTE, sTE.n), (sCON, CONSTS_LDW)], inc=sACT)
                fm_psum_tok[m % 2] = r.sems["sACT"].n
                for j in range(TCH):
                    w = [(sACT, sACT.n), (sDVE, sDVE.n)] if j == 0 else []
                    r.op("tensor", lambda e, j=j, sbuf=sbuf, PSD=PSD: e.transpose(
                        PSD[:, j * 128:(j + 1) * 128],
                        stg_pair[sbuf][:, j * 128:(j + 1) * 128], idb[:]),
                        wait=w, inc=(sTE if j == TCH - 1 else None))
                w = [(sTE, sTE.n)]
                if m == 0:
                    w += list(first_evac_wait)
                r.op("vector", lambda e, m=m, PSD=PSD: e.tensor_copy(
                    dest_tm(dest_big, m), PSD[:, 0:TOK].rearrange("p (j t) -> p j t", t=128)),
                    wait=w, inc=sDVE)
            its.append(dict(weight=weight_ap, row=m, nrows=128, body=body, post=post))
        return its

    if DEBUG[0] & 4:
        r.op("gpsimd", lambda e: e.dma_start(out=dbg_att[:, :], in_=BIG8[:, :]),
             wait=[(sOX, OEX)], inc=sST, inc_amt=16)
    run_pipelined(fm_items(Wo, YST, BIGB, first_wait=[(sOX, OEX)]), wnat3, lhsT3)

    for i in range(TCH):
        r.op("sync", lambda e, i=i: e.dma_start(out=xt3[:, :],
                                                in_=x_in[i * 128:(i + 1) * 128, :]),
             wait=[(sDVE, sDVE.n), (sPOOL, sPOOL.n)], inc=sLDX, inc_amt=16)
        src = BIGB[:, i * D:(i + 1) * D]
        ln_chain(src, xnb3[:, :], par=i,
                 extra_act=[(sDVE, sDVE.n)], extra_dve=[(sDVE, sDVE.n)])
        r.op("gpsimd", lambda e, i=i: e.tensor_add(
            BIGB[:, i * D:(i + 1) * D], xt3[:, :], xnb3[:, :]),
            wait=[(sPOOL, sPOOL.n), (sLDX, sLDX.n)], inc=sPOOL)
        r.op("gpsimd", lambda e, i=i: e.dma_start(
            out=z1d[i * 128:(i + 1) * 128, :], in_=BIGB[:, i * D:(i + 1) * D]),
            wait=[(sPOOL, sPOOL.n)], inc=sZ1, inc_amt=16)
    Z1ST = sZ1.n

    r.barrier()
    p3.close()

    p4 = contextlib.ExitStack()
    a4 = lambda name, shape, dtp: p4.enter_context(nc.sbuf_tensor(name, shape, dtp))
    wnat4 = [a4("wnat4", [128, D], F32), ]
    lhsT4 = [a4(f"lhsT4{i}", [128, KCH * 128], BF16) for i in range(2)]
    HST0 = a4("HST0", [128, TOK], BF16)
    HST = [HST0, HST0]
    xnb4 = a4("xnb4", [128, D], BF16)
    AgT = a4("AgT", [128, KCH * 68], BF16)
    BTt = a4("BTt", [64, D], BF16)
    ug = a4("ug", [68, TOK], F32)
    umask = a4("umask", [64, TOK], BF16)
    gt = a4("gt", [128, 4 * TCH], F32)
    smaxc = a4("smaxc", [128, TCH], F32)
    indt = a4("indt", [128, 4 * TCH], BF16)
    indf = a4("indf", [4, TOK], BF16)

    r.op("sync", lambda e: e.dma_start(out=wnat4[0][0:64, :], in_=Af[:, :]),
         wait=[(sTE, sTE.n)], inc=sLDW, inc_amt=16)
    r.op("sync", lambda e: e.dma_start(out=wnat4[0][64:68, :], in_=gw[:, :]),
         inc=sLDW, inc_amt=16)
    for k in range(KCH):
        kk = k % 4
        half = (k // 4) % 2
        w = [(sLDW, sLDW.n)] if k == 0 else []
        if kk == 0:
            w.append((sDVE, sDVE.n))
        r.op("tensor", lambda e, k=k, kk=kk, half=half: e.transpose(
            PSC[0:128, half * 512 + kk * 128: half * 512 + kk * 128 + 68],
            wnat4[0][0:68, k * 128:(k + 1) * 128], id32[0:68, 0:68]),
            wait=w, inc=(sTE if kk == 3 or k == KCH - 1 else None))
        if kk == 3 or k == KCH - 1:
            kg = k // 4
            nk = kk + 1
            r.op("vector", lambda e, kg=kg, half=half, nk=nk: e.tensor_copy(
                AgT[0:128, :].rearrange("p (kx q) -> p kx q", q=68)[:, kg * 4: kg * 4 + nk, :],
                PSC[:, half * 512: half * 512 + nk * 128]
                .rearrange("p (kx q) -> p kx q", q=128)[:, :, 0:68]),
                wait=[(sTE, sTE.n)], inc=sDVE)
    r.op("sync", lambda e: e.dma_start(
        out=wnat4[0][0:128, 0:64 * KCH].rearrange("p (m q) -> p m q", q=64),
        in_=Bf.rearrange("(m p) q -> p m q", p=128)),
        wait=[(sTE, sTE.n), (sDVE, sDVE.n)], inc=sLDW, inc_amt=16)
    for m in range(KCH):
        mm4 = m % 4
        half = (m // 4) % 2
        w = [(sLDW, sLDW.n)] if m == 0 else []
        if mm4 == 0:
            w.append((sDVE, sDVE.n))
        r.op("tensor", lambda e, m=m, mm4=mm4, half=half: e.transpose(
            PSC[0:64, half * 512 + mm4 * 128: half * 512 + (mm4 + 1) * 128],
            wnat4[0][:, m * 64:(m + 1) * 64], id32[:]),
            wait=w, inc=(sTE if mm4 == 3 or m == KCH - 1 else None))
        if mm4 == 3 or m == KCH - 1:
            mg = m // 4
            nm = mm4 + 1
            r.op("vector", lambda e, mg=mg, half=half, nm=nm: e.tensor_copy(
                BTt[0:64, mg * 512: mg * 512 + nm * 128],
                PSC[0:64, half * 512: half * 512 + nm * 128]),
                wait=[(sTE, sTE.n)], inc=sDVE)

    for i in range(TCH):
        src = BIGB[:, i * D:(i + 1) * D]
        ln_chain(src, xnb4[:, :], par=i,
                 extra_act=[(sPOOL, sPOOL.n)], extra_dve=[(sPOOL, sPOOL.n)],
                 extra_apply=[(sTE, sTE.n)])
        tm_to_fm(xnb4, big8_dest, i, wait_src=[(sPOOL, sPOOL.n)])

    if DEBUG[0] & 1:
        r.op("gpsimd", lambda e: e.dma_start(out=dbg_z1[:, :], in_=z1d[:, :]),
             wait=[(sZ1, Z1ST)], inc=sST, inc_amt=16)
    if DEBUG[0] & 2:
        r.op("gpsimd", lambda e: e.dma_start(out=dbg_znt[:, :], in_=BIG8[:, :]),
             wait=[(sTE, sTE.n), (sDVE, sDVE.n)], inc=sST, inc_amt=16)

    for tb in range(NTB):
        for k in range(KCH):
            w = []
            if tb == 0 and k == 0:
                w = [(sDVE, sDVE.n), (sACT, sACT.n)]
            r.op("tensor", lambda e, tb=tb, k=k: e.matmul(
                PSA[0:68, tb * PTB:(tb + 1) * PTB],
                AgT[:, k * 68:(k + 1) * 68],
                BIG8[:, k * TOK + tb * PTB: k * TOK + (tb + 1) * PTB],
                start=(k == 0), stop=(k == KCH - 1)),
                wait=w, inc=(sTE if (k == KCH - 1 and tb == NTB - 1) else None))
    r.op("vector", lambda e: e.tensor_scalar(ug[:, :], PSA[0:68, 0:TOK],
                                             gbias[0:68, 0:1], None, ALU.add),
         wait=[(sTE, sTE.n), (sCON, CONSTS_LDW)], inc=sDVE)
    for i in range(TCH):
        w = [(sDVE, sDVE.n)] if i == 0 else []
        r.op("tensor", lambda e, i=i: e.transpose(
            PSB[:, i * 4:(i + 1) * 4], ug[64:68, i * 128:(i + 1) * 128], id32[64:68, 64:68]),
            wait=w, inc=(sTE if i == TCH - 1 else None))
    r.op("scalar", lambda e: e.copy(gt[:, 0:4 * TCH], PSB[:, 0:4 * TCH]),
         wait=[(sTE, sTE.n)], inc=sACT)
    r.op("vector", lambda e: e.reduce_max(
        smaxc[:, :], gt[:, :].rearrange("p (i g) -> p i g", g=4), AXX),
        wait=[(sACT, sACT.n)], inc=sDVE)
    for i in range(TCH):
        r.op("vector", lambda e, i=i: e.tensor_single_scalar(
            indt[:, i * 4:(i + 1) * 4], gt[:, i * 4:(i + 1) * 4],
            smaxc[:, i:i + 1], ALU.is_ge),
            wait=[(sDVE, sDVE.n)], inc=sDVE)
    for i in range(TCH):
        w = [(sDVE, sDVE.n)] if i == 0 else []
        r.op("tensor", lambda e, i=i: e.transpose(
            PSDh[0][0:4, i * 128:(i + 1) * 128], indt[:, i * 4:(i + 1) * 4], idb[:]),
            wait=w, inc=(sTE if i == TCH - 1 else None))
    r.op("vector", lambda e: e.tensor_copy(indf[0:4, :], PSDh[0][0:4, 0:TOK]),
         wait=[(sTE, sTE.n)], inc=sDVE)
    for tb in range(NTB):
        r.op("tensor", lambda e, tb=tb: e.matmul(
            PSE[0:64, 0:PTB], mk[0:4, :], indf[0:4, tb * PTB:(tb + 1) * PTB],
            start=True, stop=True),
            wait=[(sDVE, sDVE.n)], inc=sTE)
        r.op("vector", lambda e, tb=tb: e.scalar_tensor_tensor(
            umask[0:64, tb * PTB:(tb + 1) * PTB], PSE[0:64, 0:PTB], 1.0,
            ug[0:64, tb * PTB:(tb + 1) * PTB], ALU.min, ALU.mult),
            wait=[(sTE, sTE.n)], inc=sDVE)
    UMASK_DVE = sDVE.n
    if DEBUG[0] & 32:
        r.op("gpsimd", lambda e: e.dma_start(out=dbg_gt[:, :], in_=gt[:, :]),
             wait=[(sDVE, UMASK_DVE)], inc=sST, inc_amt=16)
    if DEBUG[0] & 64:
        r.op("gpsimd", lambda e: e.dma_start(out=dbg_if[:, :], in_=indf[0:4, :]),
             wait=[(sDVE, UMASK_DVE)], inc=sST, inc_amt=16)
    if DEBUG[0] & 128:
        r.op("gpsimd", lambda e: e.dma_start(out=dbg_ug[:, :], in_=ug[:, :]),
             wait=[(sDVE, UMASK_DVE)], inc=sST, inc_amt=16)
    if DEBUG[0] & 16:
        r.op("gpsimd", lambda e: e.dma_start(out=dbg_um[:, :], in_=umask[0:64, :]),
             wait=[(sDVE, UMASK_DVE)], inc=sST, inc_amt=16)


    def lora_mm(m, tb, psum):
        w = [(sDVE, UMASK_DVE)] if (m == 0 and tb == 0) else []
        r.op("tensor", lambda e, m=m, tb=tb, psum=psum: e.matmul(
            psum[:, tb * PTB:(tb + 1) * PTB],
            BTt[0:64, m * 128:(m + 1) * 128],
            umask[0:64, tb * PTB:(tb + 1) * PTB],
            start=False, stop=True),
            wait=w, inc=(sTE if tb == NTB - 1 else None))

    fm_psum_tok[0] = r.sems["sACT"].n
    fm_psum_tok[1] = r.sems["sACT"].n
    run_pipelined(fm_items(Wf, HST, BIGB, bias_col=bcolv, extra_mm=lora_mm,
                           first_evac_wait=[(sZ1, Z1ST), (sPOOL, r.sems["sPOOL"].n)]),
                  wnat4, lhsT4)

    r.barrier()
    p4.close()

    if DEBUG[0] & 8:
        r.op("gpsimd", lambda e: e.dma_start(out=dbg_htm[:, :], in_=BIGB[:, :]),
             inc=sST, inc_amt=16)
    p5 = contextlib.ExitStack()
    a5 = lambda name, shape, dtp: p5.enter_context(nc.sbuf_tensor(name, shape, dtp))
    xt5 = [a5(f"xt5{i}", [128, D], BF16) for i in range(2)]
    xnb5 = a5("xnb5", [128, D], BF16)
    zout = [a5(f"zout{i}", [128, D], F32) for i in range(2)]

    for i in range(TCH):
        bufi = i % 2
        w = [(sPOOL, sPOOL.n)] if i >= 2 else []
        if i == 0:
            w = [(sZ1, Z1ST)]
        r.op("sync", lambda e, i=i, bufi=bufi: e.dma_start(
            out=xt5[bufi][:, :], in_=z1d[i * 128:(i + 1) * 128, :]),
            wait=w, inc=sLX[bufi], inc_amt=16)
        src = BIGB[:, i * D:(i + 1) * D]
        ln_chain(src, xnb5[:, :], par=i,
                 extra_act=[(sDVE, sDVE.n)], extra_dve=[(sDVE, sDVE.n)])
        w = [(sPOOL, sPOOL.n), (sLX[bufi], sLX[bufi].n)]
        if i >= 2:
            w.append((sST, sST.n))
        r.op("gpsimd", lambda e, bufi=bufi: e.tensor_add(
            zout[bufi][:, :], xt5[bufi][:, :], xnb5[:, :]),
            wait=w, inc=sPOOL)
        r.op("gpsimd", lambda e, i=i, bufi=bufi: e.dma_start(
            out=out_d[i * 128:(i + 1) * 128, :], in_=zout[bufi][:, :]),
            wait=[(sPOOL, sPOOL.n)], inc=sST, inc_amt=16)
    r.waitonly("gpsimd", [(sST, sST.n)])

    with nc.Block() as block:
        block.sync(lambda e: [f(e) for f in r.q["sync"]])
        block.scalar(lambda e: [f(e) for f in r.q["scalar"]])
        block.vector(lambda e: [f(e) for f in r.q["vector"]])
        block.tensor(lambda e: [f(e) for f in r.q["tensor"]])
        block.gpsimd(lambda e: [f(e) for f in r.q["gpsimd"]])

    return nc


def make_inputs(cfg, core, full):
    c = cfg
    TOK, D, TCH = c.TOK, c.D, c.TCH
    x = np.ascontiguousarray(full["x"].reshape(c.T, D)[core * TOK:(core + 1) * TOK])
    cos, sin = full["cos"], full["sin"]
    cosT = np.ascontiguousarray(cos[core * TOK:(core + 1) * TOK].T)
    sinT = sin[core * TOK:(core + 1) * TOK].T.copy()
    sinT[64:128, :] = -sinT[64:128, :]
    ii = np.arange(128, dtype=np.int32)
    idxK = (core * 128 + ii).reshape(128, 1)
    idxO = (8 * ii + core).reshape(128, 1)
    idxV = np.stack([core * TOK + j * 128 + ii for j in range(TCH)], axis=1).astype(np.int32)
    tri = np.triu(np.ones((128, 128), dtype=np.float32))
    mkm = np.zeros((4, 64), dtype=np.float32)
    for g, rk in enumerate(c.RANKS):
        mkm[g, :rk] = 1.0
    return {
        "x_in": x,
        "Wq": full["Wq"], "Wk": full["Wk"], "Wv": full["Wv"], "Wo": full["Wo"],
        "Wf": full["W"], "bf_in": full["b"], "Af": full["A"], "Bf": full["B"],
        "gw": full["gate_w"], "gb": full["gate_b"],
        "cosd": cosT, "sind": np.ascontiguousarray(sinT),
        "id32d": np.eye(128, dtype=np.float32),
        "idbd": np.eye(128, dtype=np.float32).astype(BFNP),
        "trid": tri.astype(BFNP),
        "mkd": mkm.astype(BFNP),
        "idxKd": idxK, "idxOd": idxO, "idxVd": np.ascontiguousarray(idxV),
    }


def run(cfg, full, trace=False):
    nc = build_program(cfg)
    in_maps = [make_inputs(cfg, corei, full) for corei in range(8)]
    res = run_bass_kernel_spmd(nc, in_maps, core_ids=list(range(8)), trace=trace)
    outs = [res.results[i]["out_d"] for i in range(8)]
    if DEBUG[0]:
        ks = [k for b, k in ((1, "dbg_z1"), (2, "dbg_znt"), (4, "dbg_att"),
                             (8, "dbg_htm"), (16, "dbg_um"), (32, "dbg_gt"),
                             (64, "dbg_if"), (128, "dbg_ug")) if DEBUG[0] & b]
        dbg = {k: [res.results[i][k] for i in range(8)] for k in ks}
        return np.concatenate(outs, axis=0), (res, dbg)
    return np.concatenate(outs, axis=0), res


def kernel(L0_attn_x, L0_attn_Wq, L0_attn_Wk, L0_attn_Wv, L0_attn_Wo,
           L0_ffn_W, L0_ffn_b, L0_ffn_A, L0_ffn_B, L0_ffn_gate_w, L0_ffn_gate_b):
    cfg = REAL
    cos, sin = rope_tables(cfg.T)
    full = {
        "x": np.asarray(L0_attn_x), "Wq": np.asarray(L0_attn_Wq),
        "Wk": np.asarray(L0_attn_Wk), "Wv": np.asarray(L0_attn_Wv),
        "Wo": np.asarray(L0_attn_Wo), "W": np.asarray(L0_ffn_W),
        "b": np.asarray(L0_ffn_b), "A": np.asarray(L0_ffn_A),
        "B": np.asarray(L0_ffn_B), "gate_w": np.asarray(L0_ffn_gate_w),
        "gate_b": np.asarray(L0_ffn_gate_b),
        "cos": cos, "sin": sin,
    }
    out, _ = run(cfg, full)
    return out.reshape(1, cfg.T, cfg.D)
